# version 05
# baseline (speedup 1.0000x reference)
import math
import contextlib
import numpy as np
import concourse.bass as bass
import concourse.mybir as mybir
from concourse.bass_utils import run_bass_kernel_spmd

F32 = mybir.dt.float32
BF16 = mybir.dt.bfloat16
AF = mybir.ActivationFunctionType
ALU = mybir.AluOpType
AX = mybir.AxisListType.X

D = 1024
NH = 16
HD = 64
NP = 8
IFF = 512
NEXP = 64
LN_EPS = 1e-5
CA = 45.0
CB = 10.0
ROPE_BASE = 10000.0
RINV = 1.0 / D


class Sem:

    def __init__(self, ctx, nc, name):
        self.h = ctx.enter_context(nc.semaphore(name))
        self.v = 0
        self.base = 0
        self.name = name


ENGINES = ("sync", "gpsimd", "tensor", "vector", "scalar")


class MK:
    def __init__(self, T=4096, phases=(1, 2, 3, 4), n_exp=NEXP, debug=False, serial=False,
                 serial_parts=()):
        self.serial = serial
        self.serial_parts = set(serial_parts)
        self.T = T
        self.NT = T // 128
        self.NC = T // 512
        self.SLAB = min(1024, T)
        self.NSLAB = T // self.SLAB
        self.ST = self.SLAB // 128
        self.SC = self.SLAB // 512
        self.phases = phases
        self.n_exp = n_exp
        self.debug = debug
        self.nc = bass.Bass("TRN2", target_bir_lowering=False)
        self.ctx = contextlib.ExitStack()
        self.ops = {e: [] for e in ENGINES}
        self.marks = {e: {} for e in ENGINES}
        self._xst = {}

    _CHAIN = {"vector": "sVE", "scalar": "sAC", "gpsimd": "sGPE"}

    def op(self, eng, fn, waits=(), inc=None, n=1):
        chain = self._CHAIN.get(eng)
        if chain is not None and not self.serial:
            cs = getattr(self, chain)
            waits = list(waits) + [(cs, cs.v)]
            if inc is None:
                inc = cs
        if self.serial:
            defsem = {"vector": "sVE", "scalar": "sAC", "tensor": "sMM",
                      "gpsimd": "sGPE"}.get(eng)
            if inc is None and defsem is not None:
                inc = getattr(self, defsem)
            waits = [(getattr(self, "s" + nm), getattr(self, "s" + nm).v)
                     for nm in ("LDX", "LDW", "LDA", "ST", "MM", "VE", "AC", "GP", "GPE")]
        wl = []
        wm = self.marks[eng]
        for s, v in waits:
            if v is None or v <= 0:
                continue
            if wm.get(s.name, 0) >= v:
                continue
            wm[s.name] = v
            pv = v - s.base
            if pv > 0:
                wl.append((s.h, pv))
        if inc is not None:
            inc.v += n
            self.ops[eng].append((fn, wl, inc.h, n))
            return inc.v
        self.ops[eng].append((fn, wl, None, 0))
        return None

    def flush(self):
        nc = self.nc
        queues = {e: self.ops[e] for e in ENGINES if self.ops[e]}
        if not queues:
            return
        self.ops = {e: [] for e in ENGINES}
        with nc.Block() as blk:
            def mk_runner(lst):
                def run(eng):
                    for fn, wl, inch, n in lst:
                        for h, v in wl:
                            eng.wait_ge(h, v)
                        instr = fn(eng)
                        if inch is not None:
                            instr.then_inc(inch, n)
                return run
            for e, lst in queues.items():
                getattr(blk, e)(mk_runner(lst))

    _uid = 0

    def _nm(self, name):
        MK._uid += 1
        return f"{name}_{MK._uid}"

    def sb(self, shape, dt, name):
        return self.ctx.enter_context(self.nc.sbuf_tensor(self._nm(name), shape, dt))[:]

    def sbl(self, stack, shape, dt, name):
        return stack.enter_context(self.nc.sbuf_tensor(self._nm(name), shape, dt))[:]

    def psum(self, stack, shape, dt, name):
        return stack.enter_context(self.nc.psum_tensor(self._nm(name), shape, dt))[:]

    def dma(self, eng, dst, src, sem, waits=()):
        self.op(eng, lambda e, d=dst, s=src: e.dma_start(d, s), waits, sem, 16)
        return sem.v

    _xst = {}

    def dma_snapshot(self):
        out = [(s, s.v) for s in (self.sLDX, self.sLDA, self.sST, self.sGP, self.sLDW)]
        for sems in getattr(self, "_streams", {}).values():
            out.extend((s, s.v) for s in sems)
        return [w for w in out if w[1] > 0]

    def dma_drain(self, engines=("sync", "gpsimd", "tensor", "vector", "scalar")):
        snap = self.dma_snapshot()
        if not snap:
            return
        for eng in engines:
            s0, v0 = snap[-1]
            pv0 = max(v0 - s0.base, 0)
            self.op(eng, lambda e, s0=s0, pv0=pv0: e.wait_ge(s0.h, pv0), snap[:-1])

    def ds(self, name, n=2):
        if not hasattr(self, "_streams"):
            self._streams = {}
        if name not in self._streams:
            self._streams[name] = [Sem(self.ctx, self.nc, f"{name}{i}") for i in range(n)]
        return self._streams[name]

    def sload(self, name, idx, eng, dst, src, waits=(), n=2):
        sems = self.ds(name, n)
        s = sems[idx % n]
        self.op(eng, lambda e, d=dst, sc=src: e.dma_start(d, sc), waits, s, 16)
        return (s, s.v)

    def declare_io(self):
        nc, T = self.nc, self.T
        d = lambda n, s, k="ExternalInput": nc.dram_tensor(n, s, F32, kind=k).ap()
        self.xin = d("x", [T, D])
        self.Wq, self.Wk, self.Wv, self.Wo = (d(n, [D, D]) for n in ("Wq", "Wk", "Wv", "Wo"))
        self.qg, self.qb, self.kg, self.kb = (d(n, [D]) for n in ("qg", "qb", "kg", "kb"))
        self.wq3, self.wk3, self.wv3 = (d(n, [NH, HD, HD]) for n in ("wq3", "wk3", "wv3"))
        self.rw = d("rw", [D, NEXP])
        self.sg, self.su, self.sd = d("sg", [D, IFF]), d("su", [D, IFF]), d("sd", [IFF, D])
        self.eg, self.eu = d("eg", [NEXP, D, IFF]), d("eu", [NEXP, D, IFF])
        self.ed = d("ed", [NEXP, IFF, D])
        self.identf = d("identf", [128, 128])
        self.cosP = d("cosP", [NP, HD, T])
        self.sinP = d("sinP", [NP, HD, T])
        self.causalf = d("causalf", [128, 128])
        self.yout = d("y", [T, D], "ExternalOutput")
        di = lambda n, s, dt=BF16: nc.dram_tensor(n, s, dt, kind="Internal").ap()
        self.xres = di("xres", [T, D], F32)
        self.qs = di("qs", [NP, 128, T])
        self.ks = di("ks", [NP, 128, T])
        self.vs = di("vs", [NP, T, 128])
        self.afm = di("afm", [NP, 128, T])
        self.lnfm = di("lnfm", [NP, 128, T])
        self.lnpm = di("lnpm", [NP, 128, T])
        self.vb = di("vb", [T, D])
        self.wfme = di("wfme", [NEXP, T], F32)
        self.dbg = {}
        if self.debug:
            for nm, shp in [("d_q", [NP, 128, T]), ("d_k", [NP, 128, T]),
                            ("d_vs", [NP, T, 128]), ("d_m", [NH, T]),
                            ("d_afm", [NP, 128, T]), ("d_x1", [T, D]),
                            ("d_lnfm", [NP, 128, T]), ("d_lnpm", [NP, 128, T]),
                            ("d_w", [NEXP, T]),
                            ("d_qb", [NP, 128, T]), ("d_kb", [NP, 128, T]),
                            ("d_vb", [T, D]), ("d_mb", [1, T]), ("d_x3", [T, D]),
                            ("d_ln1", [T, D]), ("d_gb", [128, 4 * D]), ("d_col", [128, 16]),
                            ("d_et", [128, 1024]), ("d_pvf", [65, 512]), ("d_ms", [1, 1024]),
                            ("d_idb", [128, 128]), ("d_qln", [T, D])]:
                self.dbg[nm] = nc.dram_tensor(nm, shp, F32, kind="ExternalOutput").ap()

    def alloc_global(self):
        sb = self.sb
        self.ident_f = sb([128, 128], F32, "ident_f")
        self.ident_b = sb([128, 128], BF16, "ident_b")
        self.causal_b = sb([128, 128], BF16, "causal_b")
        self.ones_col = sb([128, 1], BF16, "ones_col")
        self.ones_row = sb([1, 128], BF16, "ones_row")
        self.onesf_row = sb([1, 128], F32, "onesf_row")
        self.col = sb([128, 16], F32, "col")
        self.cstv = [LN_EPS, RINV, -0.5, 0.5, -1.0, 1e-9, 2.0, CA, CB, 1.0 / 32.0, 0.0, 1.0, -1e30]
        self.cst = sb([128, len(self.cstv)], F32, "cst")
        for i, v in enumerate(self.cstv):
            self.nc.const_aps.register(F32, v, self.cst[:, i:i + 1])
        self.scr = sb([128, D], F32, "scr")
        self.scr2 = sb([128, D], F32, "scr2")
        self.dump = sb([128, 512], F32, "dump")
        self.sq = sb([128, D], BF16, "sq")
        for name in ("LDX", "LDW", "LDA", "ST", "MM", "VE", "AC", "GP", "GPE"):
            setattr(self, "s" + name, Sem(self.ctx, self.nc, name))
        self._ln_prev = None
        self._qk_prev = None

    def act_load(self, funcs):
        from concourse.hw_specs import get_activation_tables
        tabs = get_activation_tables(self.nc.m.arch)
        for idx, fl in enumerate(tabs.values()):
            if all(f in fl for f in funcs):
                def emit(e, idx=idx):
                    return e.add_instruction(mybir.InstLoadActFuncSet(
                        name=self.nc.get_next_instruction_name(),
                        act_func_set_id=idx, ins=[], outs=[]))
                self.op("scalar", emit, inc=self.sAC)
                return
        raise ValueError(f"no table set contains {funcs}")

    def p0(self):
        self.dma("sync", self.ident_f, self.identf, self.sLDX)
        self.dma("gpsimd", self.ident_b, self.identf, self.sGP)
        self.dma("gpsimd", self.causal_b, self.causalf, self.sGP)
        for t in (self.ones_col, self.ones_row, self.onesf_row):
            self.op("gpsimd", lambda e, t=t: e.memset(t, 1.0))
        for i, v in enumerate(self.cstv):
            self.op("gpsimd", lambda e, i=i, v=v: e.memset(self.cst[:, i:i + 1], v),
                    inc=self.sGPE)
        self.act_load([AF.Exp, AF.Ln, AF.Square, AF.Identity])
        lv, gv, gev = self.sLDX.v, self.sGP.v, self.sGPE.v
        self.op("sync", lambda e: e.wait_ge(self.sLDX.h, lv))
        self.op("gpsimd", lambda e: e.wait_ge(self.sGP.h, gv))
        self.op("gpsimd", lambda e: e.wait_ge(self.sGPE.h, gev))
        self.flush()

    def ln_tile(self, src, out_bf, extra_src_waits=()):
        c = self.col
        s_, m_, t_, ss_, e2_, v_, r_ = (c[:, i:i + 1] for i in range(7))
        w = list(extra_src_waits)
        self.op("vector", lambda e: e.reduce_sum(s_, src, axis=AX), w)
        ve_mu = self.op("vector",
                        lambda e: e.tensor_scalar(m_, s_, self.cst[:, 1:2], None, ALU.mult),
                        inc=self.sVE)
        wsq = [(self.sVE, ve_mu)] + ([(self.sVE, self._ln_prev)] if self._ln_prev else [])
        self.op("scalar", lambda e: e.activation(t_, m_, AF.Square), wsq)
        h2_ = c[:, 7:8]
        self.op("scalar", lambda e: e.activation(self.dump, src[:, 0:512], AF.Square,
                                                 accum_out=ss_), w)
        ac_h = self.op("scalar", lambda e: e.activation(self.dump, src[:, 512:1024],
                                                        AF.Square, accum_out=h2_),
                       inc=self.sAC)
        ve_h = self.op("vector", lambda e: e.tensor_scalar(ss_, ss_, h2_, None, ALU.add),
                       [(self.sAC, ac_h)], self.sVE)
        ac_ss = self.op("scalar",
                        lambda e: e.activation(e2_, ss_, AF.Identity, scale=RINV),
                        [(self.sVE, ve_h)], self.sAC)
        ve_var = self.op("vector", lambda e: e.tensor_scalar(v_, e2_, t_, None, ALU.subtract),
                         [(self.sAC, ac_ss)], self.sVE)
        self._ln_prev = ve_var
        self.op("scalar", lambda e: e.activation(r_, v_, AF.Ln, bias=LN_EPS),
                [(self.sVE, ve_var)])
        ac_r = self.op("scalar", lambda e: e.activation(r_, r_, AF.Exp, scale=-0.5),
                       inc=self.sAC)
        ve_out = self.op("vector",
                         lambda e: e.tensor_scalar(out_bf, src, m_, r_, ALU.subtract, ALU.mult),
                         [(self.sAC, ac_r)], self.sVE)
        return ve_out

    class TrPool:

        def __init__(self, mk, ps_ap, slot_stride, ident, nslots=2):
            self.mk, self.ps, self.stride, self.ident, self.n = mk, ps_ap, slot_stride, ident, nslots
            self.tr_marks = []
            self.ev_marks = []

        def tr(self, src, in_waits=()):
            i = len(self.tr_marks)
            w = list(in_waits)
            if i >= self.n:
                w.append(self.ev_marks[i - self.n])
            kk, mm_ = src.shape[0], src.shape[1]
            base = (i % self.n) * self.stride
            dst = self.ps[0:mm_, base:base + kk]
            mm = self.mk.op("tensor",
                            lambda e, d=dst, s=src, kk=kk: e.transpose(
                                d, s, self.ident[0:kk, 0:kk]),
                            w, self.mk.sMM)
            self.tr_marks.append(mm)
            self.ev_marks.append(None)
            return dst, mm, i

        def done(self, i, mark):
            self.ev_marks[i] = mark

    def p1b(self):
        nc, T, NT = self.nc, self.T, self.NT
        st = contextlib.ExitStack()
        sb = lambda shape, dt, name: self.sbl(st, shape, dt, name)
        with st:
            stp = contextlib.ExitStack()
            ps_tr = self.psum(stp, [128, 2048], BF16, "ps_tr")
            ps_q = self.psum(stp, [128, 1024], F32, "ps_q")
            ps_k = self.psum(stp, [128, 1024], F32, "ps_k")
            ps_v = self.psum(stp, [128, 1024], F32, "ps_v")
            wq_sb = sb([128, NP * D], BF16, "wq_sb")
            wk_sb = sb([128, NP * D], BF16, "wk_sb")
            wv_sb = sb([128, NP * D], BF16, "wv_sb")
            gb = sb([128, 4 * D], F32, "gb")
            gbrow = sb([1, D], F32, "gbrow")
            xa = sb([128, 2 * D], F32, "xa")
            lnt = sb([128, 2 * D], BF16, "lnt")
            lnf = sb([128, 2 * NP * 128], BF16, "lnf")
            qkv = sb([128, 3 * D], BF16, "qkv")
            spq = sb([128, NP * 128], BF16, "spq")
            spk = sb([128, NP * 128], BF16, "spk")
            qn2 = sb([128, NT * 16], F32, "qn2")
            kn2 = sb([128, NT * 16], F32, "kn2")


            for w, dst in ((self.Wq, wq_sb), (self.Wk, wk_sb), (self.Wv, wv_sb)):
                self.dma("gpsimd", dst.rearrange("p (dc o) -> p dc o", dc=NP),
                         w.rearrange("(dc p) o -> p dc o", p=128), self.sGP)
            ldw = self.sGP.v
            ev_half = {}
            for i, w in enumerate((self.qg, self.qb, self.kg, self.kb)):
                ldg = self.dma("sync", gbrow, w[None, :], self.sLDX,
                               [(self.sMM, ev_half.get("mm"))])
                for half in range(2):
                    sl = slice(i * D + half * 512, i * D + (half + 1) * 512)
                    wts = [(self.sLDX, ldg)]
                    if ev_half.get(half) is not None:
                        wts.append((self.sVE, ev_half[half]))
                    mm = self.op("tensor",
                                 lambda e, sl=sl, h=half: e.matmul(
                                     ps_q[:, h * 512:(h + 1) * 512], self.onesf_row,
                                     gbrow[0:1, h * 512:(h + 1) * 512], start=True,
                                     stop=True, skip_group_check=True),
                                 wts, self.sMM)
                    ev_half[half] = self.op("vector",
                                            lambda e, sl=sl, h=half: e.tensor_copy(
                                                gb[:, sl], ps_q[:, h * 512:(h + 1) * 512]),
                                            [(self.sMM, mm)], self.sVE)
                ev_half["mm"] = mm
            self.flush()

            trp = MK.TrPool(self, ps_tr, 1024, self.ident_b)
            xfree = {}
            lnt_free = {}
            pjfree = {"q": None, "k": None, "v": None}
            spfree = {"q": None, "k": None}
            vspill = None
            qkvfree = {"q": None, "k": None, "v": None}
            for tt in range(NT):
                b = tt % 2
                xslice = xa[:, b * D:(b + 1) * D]
                w = [(self.sVE, xfree.get(tt - 2))] if tt >= 2 else []
                ldx = self.sload("xa1", tt, "sync", xslice,
                                 self.xin[tt * 128:(tt + 1) * 128, :], w)
                lslice = lnt[:, b * D:(b + 1) * D]
                lfb = lnf[:, b * NP * 128:(b + 1) * NP * 128]
                ve_ln = self.ln_tile(xslice, lslice,
                                     extra_src_waits=[ldx,
                                                      (self.sMM, lnt_free.get(tt - 2))])
                xfree[tt] = ve_ln
                if self.debug and tt == 0:
                    self.dma("gpsimd", self.dbg["d_col"], self.col, self.sGP,
                             [(self.sVE, ve_ln)])
                if self.debug:
                    self.dma("gpsimd", self.dbg["d_ln1"][tt * 128:(tt + 1) * 128, :],
                             lslice, self.sGP, [(self.sVE, ve_ln)])
                prev_proj = pjfree.get("lnf" + str(b))
                for dc in range(NP):
                    dst_ps, mm, i = trp.tr(lslice[:, dc * 128:(dc + 1) * 128],
                                           [(self.sVE, ve_ln)])
                    ev = self.op("vector",
                                 lambda e, d=dst_ps, dc=dc, lfb=lfb: e.tensor_copy(
                                     lfb[:, dc * 128:(dc + 1) * 128], d[:, 0:128]),
                                 [(self.sMM, mm)] + ([(self.sMM, prev_proj)] if prev_proj else []),
                                 self.sVE)
                    trp.done(i, (self.sVE, ev))
                lnt_free[tt] = trp.tr_marks[-1]
                lnf_ready = ev
                pj = {}
                for pi, (wsb, psd, nm) in enumerate(
                        ((wq_sb, ps_q, "q"), (wk_sb, ps_k, "k"), (wv_sb, ps_v, "v"))):
                    wts = [(self.sVE, lnf_ready), (self.sGP, ldw)]
                    if pjfree[nm] is not None:
                        wts.append(pjfree[nm])
                    for half in range(2):
                        for dc in range(NP):
                            mm = self.op("tensor",
                                         lambda e, psd=psd, h=half, dc=dc, wsb=wsb, lfb=lfb: e.matmul(
                                             psd[:, h * 512:(h + 1) * 512],
                                             lfb[:, dc * 128:(dc + 1) * 128],
                                             wsb[:, dc * D + h * 512: dc * D + (h + 1) * 512],
                                             start=(dc == 0), stop=(dc == NP - 1),
                                             skip_group_check=True),
                                         wts if (half == 0 and dc == 0) else (), self.sMM)
                    pj[nm] = mm
                pjfree["lnf" + str(b)] = pj["v"]
                for si, (psd, nm, goff) in enumerate(((ps_q, "q", 0), (ps_k, "k", 2 * D))):
                    c = self.col
                    s_, m_, t_, ss_, e2_, v_, r_ = (c[:, 8 + i:9 + i] for i in range(7))
                    ve1 = self.op("vector", lambda e, p=psd: e.reduce_sum(s_, p, axis=AX),
                                  [(self.sMM, pj[nm])], self.sVE)
                    ve_mu = self.op("vector", lambda e: e.tensor_scalar(
                        m_, s_, self.cst[:, 1:2], None, ALU.mult), inc=self.sVE)
                    wq1 = [(self.sVE, ve_mu), (self.sVE, self._qk_prev)] if self._qk_prev \
                        else [(self.sVE, ve_mu)]
                    self.op("scalar", lambda e: e.activation(t_, m_, AF.Square), wq1)
                    h2_ = self.col[:, 7:8]
                    self.op("scalar", lambda e, p=psd: e.activation(
                        self.dump, p[:, 0:512], AF.Square, accum_out=ss_),
                        [(self.sVE, ve1)])
                    ach = self.op("scalar", lambda e, p=psd: e.activation(
                        self.dump, p[:, 512:1024], AF.Square, accum_out=h2_),
                        inc=self.sAC)
                    veh = self.op("vector", lambda e: e.tensor_scalar(
                        ss_, ss_, h2_, None, ALU.add), [(self.sAC, ach)], self.sVE)
                    ac1 = self.op("scalar", lambda e: e.activation(
                        e2_, ss_, AF.Identity, scale=RINV), [(self.sVE, veh)], self.sAC)
                    vev = self.op("vector", lambda e: e.tensor_scalar(
                        v_, e2_, t_, None, ALU.subtract), [(self.sAC, ac1)], self.sVE)
                    self._qk_prev = vev
                    self.op("scalar", lambda e: e.activation(r_, v_, AF.Ln, bias=LN_EPS),
                            [(self.sVE, vev)])
                    ac2 = self.op("scalar", lambda e: e.activation(r_, r_, AF.Exp, scale=-0.5),
                                  inc=self.sAC)
                    wts = [(self.sAC, ac2)]
                    if qkvfree[nm] is not None:
                        wts.append(qkvfree[nm])
                    self.op("vector", lambda e, p=psd: e.tensor_scalar(
                        self.scr, p, m_, r_, ALU.subtract, ALU.mult), wts)
                    self.op("vector", lambda e, g=goff: e.tensor_tensor(
                        self.scr2, self.scr, gb[:, g:g + D], ALU.mult))
                    ve_ln2 = self.op("vector", lambda e, g=goff, nm=nm: e.tensor_tensor(
                        qkv[:, (0 if nm == "q" else D):(D if nm == "q" else 2 * D)],
                        self.scr2, gb[:, g + D:g + 2 * D], ALU.add), inc=self.sVE)
                    pjfree[nm] = (self.sVE, ve_ln2)
                    if self.debug and nm == "q":
                        self.dma("gpsimd", self.dbg["d_qln"][tt * 128:(tt + 1) * 128, :],
                                 qkv[:, 0:D], self.sGP, [(self.sVE, ve_ln2)])
                    acsq = self.op("scalar", lambda e, nm=nm: e.activation(
                        self.sq,
                        qkv[:, (0 if nm == "q" else D):(D if nm == "q" else 2 * D)],
                        AF.Square), [(self.sVE, ve_ln2)], self.sAC)
                    n2 = qn2 if nm == "q" else kn2
                    for h in range(NH):
                        ven = self.op("vector", lambda e, h=h, n2=n2, tt=tt: e.reduce_sum(
                            n2[:, tt * 16 + h:tt * 16 + h + 1],
                            self.sq[:, h * 64:(h + 1) * 64], axis=AX),
                            [(self.sAC, acsq)] if h == 0 else (), self.sVE)
                wts = [(self.sMM, pj["v"])]
                if qkvfree["v"] is not None:
                    wts.append(qkvfree["v"])
                self.op("vector", lambda e: e.tensor_copy(qkv[:, 2 * D:2 * D + 512],
                                                          ps_v[:, 0:512]), wts)
                ve_v = self.op("vector", lambda e: e.tensor_copy(qkv[:, 2 * D + 512:3 * D],
                                                                 ps_v[:, 512:1024]),
                               inc=self.sVE)
                pjfree["v"] = (self.sVE, ve_v)
                for si, (nm, stg, sdram) in enumerate((("q", spq, self.qs), ("k", spk, self.ks))):
                    off = 0 if nm == "q" else D
                    for dc in range(NP):
                        wts = []
                        if dc == 0:
                            wts.append(pjfree[nm])
                            if spfree[nm] is not None:
                                wts.append(spfree[nm])
                        dst_ps, mm, i = trp.tr(qkv[:, off + dc * 128: off + (dc + 1) * 128], wts)
                        ev = self.op("vector", lambda e, d=dst_ps, dc=dc, stg=stg: e.tensor_copy(
                            stg[:, dc * 128:(dc + 1) * 128], d[:, 0:128]),
                            [(self.sMM, mm)], self.sVE)
                        trp.done(i, (self.sVE, ev))
                    qkvfree[nm] = (self.sMM, mm)
                    spfree[nm] = self.sload(
                        "sp" + nm, tt, "sync",
                        sdram.rearrange("p d t -> d p t")[:, :, tt * 128:(tt + 1) * 128],
                        stg.rearrange("d (p t) -> d p t", p=NP), [(self.sVE, ev)])
                wts = [(self.sVE, ve_v)]
                if vspill is not None:
                    wts.append(vspill)
                vspill = self.sload(
                    "vsp", tt, "sync",
                    self.vs.rearrange("p t d -> t p d")[tt * 128:(tt + 1) * 128],
                    qkv[:, 2 * D:3 * D].rearrange("t (p d) -> t p d", p=NP), wts)
                qkvfree["v"] = vspill
            self.flush()

            stp.close()
            st2 = contextlib.ExitStack()
            ps_trf = self.psum(st2, [128, 1024], F32, "ps_trf")
            trf = MK.TrPool(self, ps_trf, 512, self.ident_f)
            for si, (n2, base) in enumerate(((qn2, 0), (kn2, T))):
                for tt in range(NT):
                    dst_ps, mm, i = trf.tr(n2[:, tt * 16:(tt + 1) * 16])
                    ev = self.op("vector", lambda e, d=dst_ps, tt=tt, base=base: e.tensor_copy(
                        self.nrm[0:16, base + tt * 128: base + (tt + 1) * 128],
                        d[0:16, 0:128]), [(self.sMM, mm)], self.sVE)
                    trf.done(i, (self.sVE, ev))
            mx = self.col[0:16, 14:15]
            ve_mx = self.op("vector", lambda e: e.reduce_max(
                mx, self.nrm[0:16, T:2 * T], axis=AX), [(self.sVE, ev)], self.sVE)
            ve_m2 = self.op("vector", lambda e: e.tensor_scalar(
                self.nrm[0:16, T:2 * T], self.nrm[0:16, 0:T], mx, None, ALU.mult),
                [(self.sVE, ve_mx)], self.sVE)
            self.op("scalar", lambda e: e.activation(
                self.nrm[0:16, T:2 * T], self.nrm[0:16, T:2 * T], AF.Ln),
                [(self.sVE, ve_m2)])
            ac_m = self.op("scalar", lambda e: e.activation(
                self.nrm[0:16, T:2 * T], self.nrm[0:16, T:2 * T], AF.Exp, scale=0.5),
                inc=self.sAC)
            ve_mr = self.op("vector", lambda e: e.tensor_scalar(
                self.mrow, self.nrm[0:16, T:2 * T], self.cst[0:16, 4:5],
                self.cst[0:16, 7:8], ALU.mult, ALU.add),
                [(self.sAC, ac_m)], self.sVE)
            if self.debug:
                self.dma("gpsimd", self.dbg["d_m"], self.mrow, self.sGP,
                         [(self.sVE, ve_mr)])
                self.dma("gpsimd", self.dbg["d_gb"], gb, self.sGP)
                self.dma("gpsimd", self.dbg["d_idb"], self.ident_b, self.sGP)
            self.flush()
            st2.close()

    def p1c(self):
        nc, T, NC = self.nc, self.T, self.NC
        KT = T // 128
        st = contextlib.ExitStack()
        sb = lambda shape, dt, name: self.sbl(st, shape, dt, name)
        with st:
            ps_s0 = self.psum(st, [128, 1024], F32, "ps_s0")
            ps_s1 = self.psum(st, [128, 1024], F32, "ps_s1")
            ps_s2 = self.psum(st, [128, 1024], F32, "ps_s2")
            ps_nf = ps_s2
            ps_pv = self.psum(st, [128, 512], F32, "ps_pv")
            ps_bt = self.psum(st, [128, 128], BF16, "ps_bt")
            qp = sb([128, 4 * T], BF16, "qp")
            kp = sb([128, 4 * T], BF16, "kp")
            vp = sb([128, 2 * KT * 130], BF16, "vp")
            et = sb([128, 3 * 1024], BF16, "et")
            pvf = sb([65, 512], F32, "pvf")
            at = sb([128, 128], BF16, "at")
            rc = sb([128, 1], F32, "rc")
            apm = sb([128, T], BF16, "apm")
            self.op("gpsimd", lambda e: e.memset(vp, 1.0))
            self.op("gpsimd", lambda e: e.memset(at, 0.0))
            self.op("gpsimd", lambda e: e.memset(apm, 0.0))
            self.op("gpsimd", lambda e: e.memset(kp[64:65, :], 1.0))
            self.flush()

            pair_mm = {}
            snap = self.dma_snapshot()
            apm_spill = None
            exp_marks = {}
            vpA = vp.rearrange("r (b kt c) -> r b kt c", b=2, c=130)
            for p in range(NP):
                bp = p % 2
                wts = [(self.sMM, pair_mm.get(p - 2))] + (snap if p == 0 else [])
                for hh in range(2):
                    co = (2 * bp + hh) * T
                    self.sload("ldp", p, "sync", qp[0:64, co:co + T],
                               self.qs[p][hh * 64:(hh + 1) * 64, :], wts)
                    wts = []
                    self.sload("ldp", p, "sync", qp[64:65, co:co + T],
                               self.mrow[2 * p + hh:2 * p + hh + 1, :])
                    self.sload("ldp", p, "sync", kp[0:64, co:co + T],
                               self.ks[p][hh * 64:(hh + 1) * 64, :])
                self.sload("ldp", p, "sync", vpA[:, bp, :, 0:64],
                           self.vs[p].rearrange("(kt r) c -> r kt c", r=128)[:, :, 0:64])
                lall = self.sload("ldp", p, "sync", vpA[:, bp, :, 65:129],
                                  self.vs[p].rearrange("(kt r) c -> r kt c", r=128)[:, :, 64:128])
                for hh in range(2):
                    co = (2 * bp + hh) * T
                    hs = slice(hh * 64, (hh + 1) * 64)
                    for qc in range(NC):
                        pv_start = True
                        pend = None
                        for g in range(KT // 2 + 1):
                            if g < KT // 2:
                                psg = (ps_s0, ps_s1, ps_s2)[g % 3]
                                wts = [lall]
                                if g % 3 == 2 and exp_marks.get("s2_free") is not None:
                                    wts.append(exp_marks["s2_free"])
                                prev = exp_marks.get(("slot", g % 3))
                                if prev is not None:
                                    wts.append((self.sAC, prev))
                                for kk in range(2):
                                    kt = 2 * g + kk
                                    self.op("tensor", lambda e, psg=psg, kk=kk, kt=kt,
                                            co=co, qc=qc: e.matmul(
                                                psg[:, kk * 512:(kk + 1) * 512],
                                                kp[0:65, co + kt * 128: co + (kt + 1) * 128],
                                                qp[0:65, co + qc * 512: co + (qc + 1) * 512],
                                                start=True, stop=True, skip_group_check=True),
                                            wts if kk == 0 else (), self.sMM)
                                    wts = ()
                                qk_mm = self.sMM.v
                                wts = [(self.sMM, qk_mm)]
                                pvprev = exp_marks.get(("et", g % 3))
                                if pvprev is not None:
                                    wts.append((self.sMM, pvprev))
                                ac = self.op("scalar", lambda e, psg=psg, g=g: e.activation(
                                    et[:, (g % 3) * 1024:(g % 3 + 1) * 1024], psg, AF.Exp),
                                    wts, self.sAC)
                                exp_marks[("slot", g % 3)] = ac
                            if pend is not None:
                                gp_, ac_ = pend
                                for kk in range(2):
                                    kt = 2 * gp_ + kk
                                    wts = [(self.sAC, ac_)]
                                    if pv_start and kk == 0:
                                        pe = exp_marks.get("pv_evac")
                                        if pe is not None:
                                            wts.append((self.sVE, pe))
                                    mm = self.op("tensor", lambda e, kt=kt, g=gp_, kk=kk, bp=bp,
                                                 hh=hh, st_=(pv_start and kk == 0),
                                                 sp=(kt == KT - 1): e.matmul(
                                                     ps_pv[0:65, :],
                                                     vpA[:, bp, kt, hh * 65:(hh + 1) * 65],
                                                     et[:, (g % 3) * 1024 + kk * 512:
                                                        (g % 3) * 1024 + (kk + 1) * 512],
                                                     start=st_, stop=sp, skip_group_check=True),
                                                 wts, self.sMM)
                                exp_marks[("et", gp_ % 3)] = mm
                                pv_start = False
                            if g < KT // 2:
                                pend = (g, ac)
                        ve_pv = self.op("vector", lambda e: e.tensor_copy(pvf, ps_pv[0:65, :]),
                                        [(self.sMM, mm)], self.sVE)
                        exp_marks["pv_evac"] = ve_pv
                        if self.debug and p == 0 and hh == 0 and qc == 0:
                            self.dma("gpsimd", self.dbg["d_pvf"], pvf, self.sGP,
                                     [(self.sVE, ve_pv)])
                        for i in range(4):
                            wts = [(self.sVE, ve_pv)]
                            prev_nf = exp_marks.get(("nf", i % 2))
                            if prev_nf is not None:
                                wts.append((self.sVE, prev_nf))
                            pm = exp_marks.get(("slot", 2))
                            if pm is not None:
                                wts.append((self.sAC, pm))
                            mmn = self.op("tensor", lambda e, i=i: e.transpose(
                                ps_nf[:, (i % 2) * 512:(i % 2) * 512 + 65],
                                pvf[0:65, i * 128:(i + 1) * 128],
                                self.ident_f[0:65, 0:65]), wts, self.sMM)
                            ve_rc = self.op("vector", lambda e, i=i: e.reciprocal(
                                rc, ps_nf[:, (i % 2) * 512 + 64:(i % 2) * 512 + 65]),
                                [(self.sMM, mmn)], self.sVE)
                            wts2 = []
                            prev_bt = exp_marks.get("bt")
                            if prev_bt is not None:
                                wts2.append((self.sMM, prev_bt))
                            wts2.append((self.sVE, ve_rc))
                            ve_at = self.op("vector", lambda e, i=i, hs=hs: e.tensor_scalar(
                                at[:, hs], ps_nf[:, (i % 2) * 512:(i % 2) * 512 + 64],
                                rc, None, ALU.mult), wts2, self.sVE)
                            exp_marks[("nf", i % 2)] = ve_at
                            wbt = [(self.sVE, ve_at)]
                            prev_ap = exp_marks.get("apm")
                            if prev_ap is not None:
                                wbt.append((self.sVE, prev_ap))
                            mmb = self.op("tensor", lambda e: e.transpose(
                                ps_bt, at, self.ident_b), wbt, self.sMM)
                            exp_marks["bt"] = mmb
                            wts3 = [(self.sMM, mmb)]
                            if apm_spill is not None and hh == 0 and qc == 0 and i == 0:
                                wts3.append(apm_spill)
                            ve_ap = self.op("vector", lambda e, i=i, hs=hs, qc=qc: e.tensor_copy(
                                apm[hs, qc * 512 + i * 128: qc * 512 + (i + 1) * 128],
                                ps_bt[hs, :]), wts3, self.sVE)
                            exp_marks["apm"] = ve_ap
                        exp_marks["s2_free"] = (self.sVE, ve_at)
                pair_mm[p] = self.sMM.v
                apm_spill = self.sload("afms", p, "sync", self.afm[p], apm,
                                       [(self.sVE, exp_marks["apm"])])
            self.flush()
            if self.debug:
                self.op("gpsimd", lambda e, v=self.sST.v: e.wait_ge(self.sST.h, v))
                self.dma("gpsimd", self.dbg["d_q"], self.qs, self.sGP)
                self.dma("gpsimd", self.dbg["d_k"], self.ks, self.sGP)
                self.dma("gpsimd", self.dbg["d_vs"], self.vs, self.sGP)
                self.dma("gpsimd", self.dbg["d_afm"], self.afm, self.sGP)
                self.flush()
        self.p1_stack.close()

    def p1d(self):
        nc, T, NC, NT = self.nc, self.T, self.NC, self.NT
        st = contextlib.ExitStack()
        sb = lambda shape, dt, name: self.sbl(st, shape, dt, name)
        with st:
            ps_o = self.psum(st, [128, 2 * 512], F32, "ps_o")
            ps_trf = self.psum(st, [128, 1024], F32, "ps_trf")
            ps_trb = self.psum(st, [128, 2048], BF16, "ps_trb")
            wo_sb = sb([128, NP * D], BF16, "wo_sb")
            af = sb([128, 2 * NP * 512], BF16, "af")
            of = sb([128, NP * 512], F32, "of")
            xa = sb([128, 2 * D], F32, "xa")
            xn = sb([128, 2 * D], F32, "xn")
            lnt = sb([128, D], BF16, "lnt")
            spl = sb([128, NP * 128], BF16, "spl")
            ldwo = self.dma("gpsimd", wo_sb.rearrange("p (dc o) -> p dc o", dc=NP),
                            self.Wo.rearrange("(dc p) o -> p dc o", p=128), self.sGP)
            trf = MK.TrPool(self, ps_trf, 512, self.ident_f)
            trb = MK.TrPool(self, ps_trb, 1024, self.ident_b)
            snap = self.dma_snapshot()
            af_mm = {}
            of_free = {}
            spl_dma = None
            xn_free = {}
            for tc in range(NC):
                b = tc % 2
                wts = [(self.sMM, af_mm.get(tc - 2))] + (snap if tc == 0 else [])
                lda = self.sload("af2", tc, "sync",
                                 af.rearrange("d (b p t) -> d b p t", b=2, p=NP)[:, b],
                                 self.afm.rearrange("p d t -> d p t")[:, :, tc * 512:(tc + 1) * 512],
                                 wts)
                for oc in range(NP):
                    wts = [lda, (self.sGP, ldwo)]
                    if of_free.get(oc % 2) is not None:
                        wts.append((self.sAC, of_free[oc % 2]))
                    for dc in range(NP):
                        mm = self.op("tensor", lambda e, oc=oc, dc=dc, b=b: e.matmul(
                            ps_o[:, (oc % 2) * 512:(oc % 2 + 1) * 512],
                            wo_sb[:, dc * D + oc * 128: dc * D + oc * 128 + 128],
                            af[:, (b * NP + dc) * 512:(b * NP + dc + 1) * 512],
                            start=(dc == 0), stop=(dc == NP - 1), skip_group_check=True),
                            wts if dc == 0 else (), self.sMM)
                    ac_ev = self.op("scalar", lambda e, oc=oc: e.activation(
                        of[:, oc * 512:(oc + 1) * 512], ps_o[:, (oc % 2) * 512:(oc % 2 + 1) * 512],
                        AF.Identity), [(self.sMM, mm)], self.sAC)
                    of_free[oc % 2] = ac_ev
                af_mm[tc] = self.sMM.v
                ac_all = self.sAC.v
                for ts_ in range(4):
                    tt = tc * 4 + ts_
                    bx = tt % 2
                    wx = [(self.sVE, xn_free.get(tt - 2))]
                    ldx = self.sload("xa1", tt, "sync", xa[:, bx * D:(bx + 1) * D],
                                     self.xin[tt * 128:(tt + 1) * 128, :], wx)
                    for oc in range(NP):
                        dst_ps, mm, i = trf.tr(of[:, oc * 512 + ts_ * 128: oc * 512 + (ts_ + 1) * 128],
                                               [(self.sAC, ac_all)])
                        wxn = [(self.sMM, mm), ldx]
                        if oc == 0 and self._xst.get(tt - 2) is not None:
                            wxn.append(self._xst[tt - 2])
                        ev = self.op("vector", lambda e, d=dst_ps, oc=oc, bx=bx: e.tensor_tensor(
                            xn[:, bx * D + oc * 128: bx * D + (oc + 1) * 128],
                            xa[:, bx * D + oc * 128: bx * D + (oc + 1) * 128],
                            d[:, 0:128], ALU.add),
                            wxn, self.sVE)
                        trf.done(i, (self.sVE, ev))
                    self._xst[tt] = self.sload("xst", tt, "sync",
                                               self.xres[tt * 128:(tt + 1) * 128, :],
                                               xn[:, bx * D:(bx + 1) * D], [(self.sVE, ev)])
                    ve_ln = self.ln_tile(xn[:, bx * D:(bx + 1) * D], lnt,
                                         extra_src_waits=[(self.sVE, ev)])
                    for dc in range(NP):
                        wts = []
                        if dc == 0 and spl_dma is not None:
                            wts.append(spl_dma)
                        dst_ps, mm, i = trb.tr(lnt[:, dc * 128:(dc + 1) * 128],
                                               [(self.sVE, ve_ln)] + wts)
                        ev2 = self.op("vector", lambda e, d=dst_ps, dc=dc: e.tensor_copy(
                            spl[:, dc * 128:(dc + 1) * 128], d[:, 0:128]),
                            [(self.sMM, mm)], self.sVE)
                        trb.done(i, (self.sVE, ev2))
                    xn_free[tt] = ve_ln
                    spl_dma = self.sload(
                        "spl", tt, "sync",
                        self.lnfm.rearrange("p d t -> d p t")[:, :, tt * 128:(tt + 1) * 128],
                        spl.rearrange("d (p t) -> d p t", p=NP), [(self.sVE, ev2)])
            self.flush()
            if self.debug:
                self.op("gpsimd", lambda e, v=self.sST.v: e.wait_ge(self.sST.h, v))
                self.dma("gpsimd", self.dbg["d_x1"], self.xres, self.sGP)
                self.dma("gpsimd", self.dbg["d_lnfm"], self.lnfm, self.sGP)
                self.flush()

    def all_sems(self):
        out = [getattr(self, "s" + nm) for nm in
               ("LDX", "LDW", "LDA", "ST", "MM", "VE", "AC", "GP", "GPE")]
        for sems in getattr(self, "_streams", {}).values():
            out.extend(sems)
        return out

    def reset_sems(self):
        self.dma_drain()
        self.flush()
        sems = [self.sMM, self.sVE, self.sAC, self.sGPE]
        for s in sems:
            self.op("sync", lambda e, h=s.h: e.sem_clear(h))
        self.flush()
        for s in sems:
            s.base = s.v

    def _phase_gap(self):
        self.reset_sems()

    def build(self):
        self.declare_io()
        with self.ctx:
            self.alloc_global()
            self.p0()
            if 1 in self.phases:
                self.p1_stack = contextlib.ExitStack()
                self.nrm = self.sbl(self.p1_stack, [16, 2 * self.T], F32, "nrm")
                self.mrow = self.sbl(self.p1_stack, [16, self.T], BF16, "mrow")
                self.p1b()
                self._phase_gap()
                self.p1c()
                self._phase_gap()
                self.p1d()
                self._phase_gap()
            if 2 in self.phases:
                self.ffn(first=True)
                self._phase_gap()
            if 3 in self.phases:
                self.p3()
                self._phase_gap()
            if 4 in self.phases:
                self.ffn(first=False)
            self.finalize()
        return self.nc

    def ffn(self, first):
        nc, T = self.nc, self.T
        SLAB, NSLAB, ST, SC = self.SLAB, self.NSLAB, self.ST, self.SC
        st = contextlib.ExitStack()
        sb = lambda shape, dt, name: self.sbl(st, shape, dt, name)
        with st:
            ps_gu = self.psum(st, [128, 2048], F32, "ps_gu")
            ps_d = self.psum(st, [128, 1024], F32, "ps_d")
            ps_m = self.psum(st, [128, 512], F32, "ps_m")
            ps_tb = self.psum(st, [128, 1024], BF16, "ps_tb")
            lnslab = sb([128, NP * SLAB], BF16, "lnslab")
            rw_b = sb([128, NP * 64], BF16, "rw_b")
            rt = sb([128, 6 * 64], F32, "rt")
            rc = sb([128, 8], F32, "rc")
            w_fm = sb([64, SLAB], BF16, "w_fm")
            wrow = sb([1, 2 * SLAB], BF16, "wrow")
            wg_sb = sb([128, 2 * 8 * 512], BF16, "wg_sb")
            wu_sb = sb([128, 2 * 8 * 512], BF16, "wu_sb")
            wd_sb = sb([128, 2 * 4 * 1024], BF16, "wd_sb")
            sgate = sb([128, 2 * 512], BF16, "sgate")
            msk = sb([128, SC * 512], BF16, "msk")
            act_fm = sb([128, 2 * 4 * SLAB], BF16, "act_fm")
            acc = sb([128, ST * 1024], F32, "acc")
            xa = sb([128, 2 * D], F32, "xa")
            xn = sb([128, 2 * D], F32, "xn")
            lnt = sb([128, D], BF16, "lnt")
            prm = sb([128, D], BF16, "prm")
            spl = sb([128, NP * 128], BF16, "spl")
            AFsig = AF.Sigmoid
            AFsilu = AF.Silu
            x_src = self.xres
            x_dst = self.xres if first else self.yout
            ln_dst = self.lnpm if first else None

            snap = self.dma_snapshot()
            ldrw = self.dma("gpsimd", rw_b.rearrange("p (dc e) -> p dc e", dc=NP),
                            self.rw.rearrange("(dc p) e -> p dc e", p=128), self.sGP,
                            snap)
            slab_mm_done = None
            wrow_dma = None
            xn_marks = {}
            for s in range(NSLAB):
                wts = snap + [(self.sMM, slab_mm_done)]
                lds = self.sload("lns", 0, "sync",
                                 lnslab.rearrange("p (dc t) -> p dc t", dc=NP),
                                 self.lnfm.rearrange("p d t -> d p t")[:, :, s * SLAB:(s + 1) * SLAB],
                                 wts, n=1)
                self.act_load([AFsig, AF.Identity])
                aff0, affw, eqm, k01, wunn, wfin = (rt[:, i * 64:(i + 1) * 64] for i in range(6))
                wfm_mark = None
                for t_ in range(ST):
                    wts = [lds, (self.sGP, ldrw)]
                    if wfm_mark is not None:
                        wts.append((self.sVE, wfm_mark))
                    for dc in range(NP):
                        mmr = self.op("tensor", lambda e, dc=dc, t_=t_, s=s: e.matmul(
                            ps_m[:, 0:64],
                            lnslab[:, dc * SLAB + t_ * 128: dc * SLAB + (t_ + 1) * 128],
                            rw_b[:, dc * 64:(dc + 1) * 64],
                            start=(dc == 0), stop=(dc == NP - 1), skip_group_check=True),
                            wts if dc == 0 else (), self.sMM)
                    aca = self.op("scalar", lambda e: e.activation(aff0, ps_m[:, 0:64], AFsig),
                                  [(self.sMM, mmr)], self.sAC)
                    vew = self.op("vector", lambda e: e.tensor_copy(affw, aff0),
                                  [(self.sAC, aca)], self.sVE)
                    for it in range(8):
                        vem = self.op("vector", lambda e: e.reduce_max(
                            rc[:, 0:1], affw, axis=AX), inc=self.sVE)
                        vee = self.op("vector", lambda e: e.tensor_scalar(
                            eqm, affw, rc[:, 0:1], self.cst[:, 12:13],
                            ALU.is_equal, ALU.mult), [(self.sVE, vem)], self.sVE)
                        vew = self.op("vector", lambda e: e.tensor_tensor(
                            affw, affw, eqm, ALU.add), [(self.sVE, vee)], self.sVE)
                    vek = self.op("vector", lambda e: e.tensor_scalar(
                        k01, aff0, rc[:, 0:1], None, ALU.is_ge), [(self.sVE, vew)], self.sVE)
                    veu = self.op("vector", lambda e: e.tensor_tensor(
                        wunn, aff0, k01, ALU.mult), [(self.sVE, vek)], self.sVE)
                    ves = self.op("vector", lambda e: e.reduce_sum(
                        rc[:, 1:2], wunn, axis=AX), [(self.sVE, veu)], self.sVE)
                    ves2 = self.op("vector", lambda e: e.tensor_scalar(
                        rc[:, 2:3], rc[:, 1:2], self.cst[:, 5:6], None, ALU.add),
                        [(self.sVE, ves)], self.sVE)
                    ver = self.op("vector", lambda e: e.reciprocal(rc[:, 3:4], rc[:, 2:3]),
                                  [(self.sVE, ves2)], self.sVE)
                    vef = self.op("vector", lambda e: e.tensor_scalar(
                        wfin, wunn, rc[:, 3:4], None, ALU.mult), [(self.sVE, ver)], self.sVE)
                    wts = [(self.sVE, vef), (self.sAC, aca)]
                    if wrow_dma is not None:
                        wts.append(wrow_dma)
                    mmt = self.op("tensor", lambda e: e.transpose(
                        ps_m[0:64, 64:192], wfin, self.ident_f), wts, self.sMM)
                    wfm_mark = self.op("vector", lambda e, t_=t_: e.tensor_copy(
                        w_fm[:, t_ * 128:(t_ + 1) * 128], ps_m[0:64, 64:192]),
                        [(self.sMM, mmt)], self.sVE)
                if self.debug:
                    self.dma("gpsimd", self.dbg["d_w"][:, s * SLAB:(s + 1) * SLAB],
                             w_fm, self.sGP, [(self.sVE, wfm_mark)])
                self.act_load([AFsilu, AF.Identity])
                mmE = {}
                slot_marks = {}
                for e_i in range(-1, self.n_exp):
                    be = (e_i + 1) % 2
                    wts = [(self.sMM, mmE.get(e_i - 2))]
                    if e_i < 0:
                        srcs = (self.sg, self.su, self.sd)
                    else:
                        srcs = (self.eg[e_i], self.eu[e_i], self.ed[e_i])
                    ldg = self.sload("wg", be, "gpsimd",
                                      wg_sb.rearrange("p (b dc i) -> p b dc i", b=2, dc=NP)[:, be],
                                      srcs[0].rearrange("(dc p) i -> p dc i", p=128), wts)
                    ldu = self.sload("wu", be, "gpsimd",
                                     wu_sb.rearrange("p (b dc i) -> p b dc i", b=2, dc=NP)[:, be],
                                     srcs[1].rearrange("(dc p) i -> p dc i", p=128))
                    ldd = self.sload("wd", be, "gpsimd",
                                     wd_sb.rearrange("p (b ic h) -> p b ic h", b=2, ic=4)[:, be],
                                     srcs[2].rearrange("(ic p) h -> p ic h", p=128))
                    if e_i >= 0:
                        wrow_dma = self.sload("wr", be, "sync",
                                              wrow[0:1, be * SLAB:(be + 1) * SLAB],
                                              w_fm[e_i:e_i + 1, :],
                                              [(self.sVE, wfm_mark), (self.sMM, mmE.get(e_i - 2))])
                    for tc in range(SC):
                        if e_i >= 0:
                            wts = [wrow_dma]
                            pm = slot_marks.get(("msk", tc))
                            if pm is not None:
                                wts.append((self.sVE, pm))
                            mmm = self.op("tensor", lambda e, tc=tc, be=be: e.matmul(
                                ps_m[:, 0:512], self.ones_row,
                                wrow[0:1, be * SLAB + tc * 512: be * SLAB + (tc + 1) * 512],
                                start=True, stop=True, skip_group_check=True),
                                wts, self.sMM)
                            vemk = self.op("vector", lambda e, tc=tc: e.tensor_copy(
                                msk[:, tc * 512:(tc + 1) * 512], ps_m[:, 0:512]),
                                [(self.sMM, mmm)], self.sVE)
                            slot_marks[("msk", tc)] = vemk
                        for ic in range(4):
                            gslot = ic % 2
                            uslot = 2 + ic % 2
                            wts = [ldg]
                            pm = slot_marks.get(("g", gslot))
                            if pm is not None:
                                wts.append((self.sAC, pm))
                            for dc in range(NP):
                                mmg = self.op("tensor", lambda e, dc=dc, ic=ic, tc=tc, be=be,
                                              gs=gslot: e.matmul(
                                    ps_gu[:, gs * 512:(gs + 1) * 512],
                                    wg_sb[:, (be * NP + dc) * 512 + ic * 128:
                                          (be * NP + dc) * 512 + (ic + 1) * 128],
                                    lnslab[:, dc * SLAB + tc * 512: dc * SLAB + (tc + 1) * 512],
                                    start=(dc == 0), stop=(dc == NP - 1),
                                    skip_group_check=True),
                                    wts if dc == 0 else (), self.sMM)
                            wts = [ldu]
                            pm = slot_marks.get(("u", uslot))
                            if pm is not None:
                                wts.append((self.sVE, pm))
                            for dc in range(NP):
                                mmu = self.op("tensor", lambda e, dc=dc, ic=ic, tc=tc, be=be,
                                              us=uslot: e.matmul(
                                    ps_gu[:, us * 512:(us + 1) * 512],
                                    wu_sb[:, (be * NP + dc) * 512 + ic * 128:
                                          (be * NP + dc) * 512 + (ic + 1) * 128],
                                    lnslab[:, dc * SLAB + tc * 512: dc * SLAB + (tc + 1) * 512],
                                    start=(dc == 0), stop=(dc == NP - 1),
                                    skip_group_check=True),
                                    wts if dc == 0 else (), self.sMM)
                            wsg = [(self.sMM, mmg)]
                            pm = slot_marks.get(("u", 2 + gslot))
                            if pm is not None:
                                wsg.append((self.sVE, pm))
                            acg = self.op("scalar", lambda e, gs=gslot, ic=ic: e.activation(
                                sgate[:, gs * 512:(gs + 1) * 512],
                                ps_gu[:, gs * 512:(gs + 1) * 512], AFsilu),
                                wsg, self.sAC)
                            slot_marks[("g", gslot)] = acg
                            veau = self.op("vector", lambda e, us=uslot, gs=gslot: e.tensor_tensor(
                                self.sq[:, 0:512], ps_gu[:, us * 512:(us + 1) * 512],
                                sgate[:, gs * 512:(gs + 1) * 512], ALU.mult),
                                [(self.sMM, mmu), (self.sAC, acg)], self.sVE)
                            slot_marks[("u", uslot)] = veau
                            wts = [(self.sVE, veau)]
                            if e_i >= 0:
                                wts.append((self.sVE, slot_marks[("msk", tc)]))
                            pm = slot_marks.get(("act", be))
                            if pm is not None and tc == 0 and ic == 0:
                                wts.append(pm)
                            if e_i >= 0:
                                veact = self.op("vector", lambda e, be=be, ic=ic, tc=tc: e.tensor_tensor(
                                    act_fm[:, (be * 4 + ic) * SLAB + tc * 512:
                                           (be * 4 + ic) * SLAB + (tc + 1) * 512],
                                    self.sq[:, 0:512], msk[:, tc * 512:(tc + 1) * 512],
                                    ALU.mult), wts, self.sVE)
                            else:
                                veact = self.op("vector", lambda e, be=be, ic=ic, tc=tc: e.tensor_copy(
                                    act_fm[:, (be * 4 + ic) * SLAB + tc * 512:
                                           (be * 4 + ic) * SLAB + (tc + 1) * 512],
                                    self.sq[:, 0:512]), wts, self.sVE)
                        slot_marks[("acttc", tc)] = veact
                    for t_ in range(ST):
                        tc = t_ // (ST // SC) if SC > 1 else 0
                        for hc in range(2):
                            dslot = (t_ * 2 + hc) % 2
                            wts = [(self.sVE, slot_marks[("acttc", tc)]), ldd]
                            pm = slot_marks.get(("d", dslot))
                            if pm is not None:
                                wts.append((self.sVE, pm))
                            for ic in range(4):
                                mmd = self.op("tensor", lambda e, ic=ic, t_=t_, hc=hc, be=be,
                                              ds=dslot: e.matmul(
                                    ps_d[:, ds * 512:(ds + 1) * 512],
                                    act_fm[:, (be * 4 + ic) * SLAB + t_ * 128:
                                           (be * 4 + ic) * SLAB + (t_ + 1) * 128],
                                    wd_sb[:, (be * 4 + ic) * 1024 + hc * 512:
                                          (be * 4 + ic) * 1024 + (hc + 1) * 512],
                                    start=(ic == 0), stop=(ic == 3),
                                    skip_group_check=True),
                                    wts if ic == 0 else (), self.sMM)
                            if e_i < 0:
                                ved = self.op("vector", lambda e, t_=t_, hc=hc, ds=dslot: e.tensor_copy(
                                    acc[:, t_ * 1024 + hc * 512: t_ * 1024 + (hc + 1) * 512],
                                    ps_d[:, ds * 512:(ds + 1) * 512]),
                                    [(self.sMM, mmd)], self.sVE)
                            else:
                                ved = self.op("vector", lambda e, t_=t_, hc=hc, ds=dslot: e.tensor_tensor(
                                    acc[:, t_ * 1024 + hc * 512: t_ * 1024 + (hc + 1) * 512],
                                    acc[:, t_ * 1024 + hc * 512: t_ * 1024 + (hc + 1) * 512],
                                    ps_d[:, ds * 512:(ds + 1) * 512], ALU.add),
                                    [(self.sMM, mmd)], self.sVE)
                            slot_marks[("d", dslot)] = ved
                    slot_marks[("act", be)] = (self.sMM, self.sMM.v)
                    mmE[e_i] = self.sMM.v
                acc_done = ved
                self.act_load([AF.Exp, AF.Ln, AF.Square, AF.Identity])
                trb = MK.TrPool(self, ps_tb, 1024, self.ident_b, nslots=1)
                spl_dma = None
                for t_ in range(ST):
                    tt = s * ST + t_
                    bx = t_ % 2
                    wts = [(self.sVE, xn_marks.get(tt - 2))]
                    ldx = self.sload("xa1", tt, "sync", xa[:, bx * D:(bx + 1) * D],
                                     x_src[tt * 128:(tt + 1) * 128, :], wts)
                    wxn = [ldx, (self.sVE, acc_done)]
                    if self._xst.get(tt - 2) is not None:
                        wxn.append(self._xst[tt - 2])
                    vex = self.op("vector", lambda e, bx=bx, t_=t_: e.tensor_tensor(
                        xn[:, bx * D:(bx + 1) * D], xa[:, bx * D:(bx + 1) * D],
                        acc[:, t_ * 1024:(t_ + 1) * 1024], ALU.add),
                        wxn, self.sVE)
                    self._xst[tt] = self.sload("xst", tt, "sync",
                                               x_dst[tt * 128:(tt + 1) * 128, :],
                                               xn[:, bx * D:(bx + 1) * D], [(self.sVE, vex)])
                    if ln_dst is None:
                        xn_marks[tt] = vex
                        continue
                    ve_ln = self.ln_tile(xn[:, bx * D:(bx + 1) * D], lnt,
                                         extra_src_waits=[(self.sVE, vex)])
                    xn_marks[tt] = ve_ln
                    ve_pm = self.op("vector", lambda e: e.tensor_copy(
                        prm.rearrange("p (h di) -> p h di", h=NH),
                        lnt.rearrange("p (di h) -> p h di", h=NH)),
                        [(self.sVE, ve_ln)], self.sVE)
                    for dc in range(NP):
                        wts2 = [(self.sVE, ve_pm)]
                        if dc == 0 and spl_dma is not None:
                            wts2.append(spl_dma)
                        dst_ps, mm, i = trb.tr(prm[:, dc * 128:(dc + 1) * 128], wts2)
                        ev2 = self.op("vector", lambda e, d=dst_ps, dc=dc: e.tensor_copy(
                            spl[:, dc * 128:(dc + 1) * 128], d), [(self.sMM, mm)], self.sVE)
                        trb.done(i, (self.sVE, ev2))
                    spl_dma = self.sload(
                        "spl", tt, "sync",
                        ln_dst.rearrange("p d t -> d p t")[:, :, tt * 128:(tt + 1) * 128],
                        spl.rearrange("d (p t) -> d p t", p=NP), [(self.sVE, ev2)])
                slab_mm_done = self.sMM.v
                self.flush()
            if self.debug:
                self.op("gpsimd", lambda e, v=self.sST.v: e.wait_ge(self.sST.h, v))
                if first:
                    self.dma("gpsimd", self.dbg["d_x1"], self.xres, self.sGP)
                    self.dma("gpsimd", self.dbg["d_lnpm"], self.lnpm, self.sGP)
                self.flush()

    def p3(self):
        nc, T, NC, NT = self.nc, self.T, self.NC, self.NT
        KT = T // 128
        st = contextlib.ExitStack()
        sb = lambda shape, dt, name: self.sbl(st, shape, dt, name)
        with st:
            q_fm = sb([128, NP * T], BF16, "q_fm")
            k_fm = sb([128, NP * T], BF16, "k_fm")
            mrb = sb([1, T], BF16, "mrb")
            self.act_load([AF.Exp, AF.Ln, AF.Square, AF.Identity])

            _sv = self.serial
            self.serial = _sv or ("mix" in self.serial_parts) or ("mixmm" in self.serial_parts)
            with contextlib.ExitStack() as stm:
                sbm = lambda shape, dt, name: self.sbl(stm, shape, dt, name)
                ps_a = self.psum(stm, [128, 2048], F32, "ps_a")
                ps_tb = self.psum(stm, [128, 2048], BF16, "ps_tb")
                lnp = sbm([128, 2 * T], BF16, "lnp")
                w3 = sbm([128, 3 * NH * 128], BF16, "w3")
                cs = sbm([128, 2 * 2 * 512], F32, "cs")
                vst = sbm([128, 512], BF16, "vst")
                vfm = sbm([128, 512], BF16, "vfm")
                snap = self.dma_snapshot()
                for ki, wsrc in enumerate((self.wq3, self.wk3, self.wv3)):
                    for hb in range(2):
                        for dd in range(2):
                            self.dma("gpsimd",
                                     w3.rearrange("p (k h d) -> p k h d", k=3, h=NH)[
                                         hb * 64:(hb + 1) * 64, ki, :, dd * 64:(dd + 1) * 64],
                                     wsrc.rearrange("h di do -> di h do"),
                                     self.sGP, snap)
                ldw3 = self.sGP.v
                veq = self.op("vector", lambda e: e.tensor_scalar(
                    w3[:, 0:NH * 128], w3[:, 0:NH * 128], self.cst[:, 9:10], None,
                    ALU.mult), [(self.sGP, ldw3)], self.sVE)
                lnp_mark = {}
                vdma = None
                q2_mark = None
                for p in range(NP):
                    bp = p % 2
                    wts = snap + [(self.sVE, veq)]
                    if p >= 2:
                        wts.append((self.sMM, lnp_mark[p - 2]))
                    ldp = self.sload("lnp", p, "sync", lnp[:, bp * T:(bp + 1) * T],
                                     self.lnpm[p], wts)
                    for tc in range(NC):
                        bc = tc % 2
                        lcs = None
                        csw = []
                        pm = lnp_mark.get(("cs", bc))
                        if pm is not None:
                            csw.append((self.sVE, pm))
                        for hb in range(2):
                            self.sload("cs", tc, "sync",
                                       cs[hb * 64:(hb + 1) * 64, bc * 1024:bc * 1024 + 512],
                                       self.cosP[p, :, tc * 512:(tc + 1) * 512], csw)
                            csw = []
                            lcs = self.sload("cs", tc, "sync",
                                             cs[hb * 64:(hb + 1) * 64,
                                                bc * 1024 + 512:bc * 1024 + 1024],
                                             self.sinP[p, :, tc * 512:(tc + 1) * 512])
                        for ki, dst in ((0, q_fm), (1, k_fm)):
                            sA = (tc % 2) * 1024
                            sB = sA + 512
                            wts = [ldp]
                            pm = lnp_mark.get(("rope", tc % 2))
                            if pm is not None:
                                wts.append((self.sVE, pm))
                            mmA = self.op("tensor", lambda e, ki=ki, p=p, tc=tc, sA=sA, bp=bp: e.matmul(
                                ps_a[:, sA:sA + 512],
                                w3[0:64, (ki * NH + 2 * p) * 128:(ki * NH + 2 * p) * 128 + 128],
                                lnp[0:64, bp * T + tc * 512: bp * T + (tc + 1) * 512],
                                start=True, stop=True, skip_group_check=True),
                                wts, self.sMM)
                            mmB = self.op("tensor", lambda e, ki=ki, p=p, tc=tc, sB=sB, bp=bp: e.matmul(
                                ps_a[:, sB:sB + 512],
                                w3[64:128, (ki * NH + 2 * p + 1) * 128:(ki * NH + 2 * p + 1) * 128 + 128],
                                lnp[64:128, bp * T + tc * 512: bp * T + (tc + 1) * 512],
                                start=True, stop=True, skip_group_check=True),
                                (), self.sMM)
                            _s2 = self.serial
                            self.serial = _s2 or ("rope" in self.serial_parts)
                            co0 = cs[0:64, bc * 1024:bc * 1024 + 512]
                            co1 = cs[64:128, bc * 1024:bc * 1024 + 512]
                            si0 = cs[0:64, bc * 1024 + 512:bc * 1024 + 1024]
                            si1 = cs[64:128, bc * 1024 + 512:bc * 1024 + 1024]
                            A0, A1 = ps_a[0:64, sA:sA + 512], ps_a[64:128, sA:sA + 512]
                            B0, B1 = ps_a[0:64, sB:sB + 512], ps_a[64:128, sB:sB + 512]
                            dlo = dst[0:64, p * T + tc * 512: p * T + (tc + 1) * 512]
                            dhi = dst[64:128, p * T + tc * 512: p * T + (tc + 1) * 512]
                            wts2 = [(self.sMM, mmB), lcs]
                            self.op("vector", lambda e, A0=A0, co0=co0: e.tensor_tensor(
                                self.scr[0:64, 0:512], A0, co0, ALU.mult), wts2)
                            self.op("vector", lambda e, B0=B0, si0=si0: e.tensor_tensor(
                                self.scr2[0:64, 0:512], B0, si0, ALU.mult))
                            self.op("vector", lambda e, dlo=dlo: e.tensor_tensor(
                                dlo, self.scr[0:64, 0:512], self.scr2[0:64, 0:512],
                                ALU.subtract))
                            self.op("vector", lambda e, A1=A1, si1=si1: e.tensor_tensor(
                                self.scr[64:128, 0:512], A1, si1, ALU.mult))
                            self.op("vector", lambda e, B1=B1, co1=co1: e.tensor_tensor(
                                self.scr2[64:128, 0:512], B1, co1, ALU.mult))
                            ve_r = self.op("vector", lambda e, dhi=dhi: e.tensor_tensor(
                                dhi, self.scr[64:128, 0:512], self.scr2[64:128, 0:512],
                                ALU.add), inc=self.sVE)
                            lnp_mark[("rope", tc % 2)] = ve_r
                            lnp_mark[("cs", tc % 2)] = ve_r
                            self.serial = _s2
                        sV = (tc % 2) * 1024
                        _s3 = self.serial
                        self.serial = _s3 or ("vmix" in self.serial_parts)
                        wts = [ldp, (self.sVE, lnp_mark.get(("rope", tc % 2)))]
                        mva = self.op("tensor", lambda e, p=p, tc=tc, sV=sV, bp=bp: e.matmul(
                            ps_a[:, sV:sV + 512],
                            w3[0:64, (2 * NH + 2 * p) * 128:(2 * NH + 2 * p) * 128 + 128],
                            lnp[0:64, bp * T + tc * 512: bp * T + (tc + 1) * 512],
                            start=True, stop=True, skip_group_check=True),
                            [w for w in wts if w[1] is not None], self.sMM)
                        mvb = self.op("tensor", lambda e, p=p, tc=tc, sV=sV, bp=bp: e.matmul(
                            ps_a[:, sV + 512:sV + 1024],
                            w3[64:128, (2 * NH + 2 * p + 1) * 128:(2 * NH + 2 * p + 1) * 128 + 128],
                            lnp[64:128, bp * T + tc * 512: bp * T + (tc + 1) * 512],
                            start=True, stop=True, skip_group_check=True),
                            (), self.sMM)
                        wts = [(self.sMM, mvb)]
                        pm = lnp_mark.get(("vfm", tc % 2))
                        if pm is not None:
                            wts.append((self.sMM, pm))
                        ve1 = self.op("vector", lambda e, sV=sV: e.tensor_copy(
                            vfm[0:64, 0:512], ps_a[0:64, sV:sV + 512]), wts, self.sVE)
                        ve2 = self.op("vector", lambda e, sV=sV: e.tensor_copy(
                            vfm[64:128, 0:512], ps_a[64:128, sV + 512:sV + 1024]),
                            inc=self.sVE)
                        lnp_mark[("rope", tc % 2)] = ve2
                        for sub in range(4):
                            wts = [(self.sVE, ve2)]
                            if vdma is not None and sub == 0:
                                wts.append(vdma)
                            pm = lnp_mark.get(("vtb", sub % 2))
                            if pm is not None:
                                wts.append((self.sVE, pm))
                            mtv = self.op("tensor", lambda e, sub=sub: e.transpose(
                                ps_tb[:, (sub % 2) * 1024:(sub % 2) * 1024 + 128],
                                vfm[:, sub * 128:(sub + 1) * 128], self.ident_b),
                                wts, self.sMM)
                            vev = self.op("vector", lambda e, sub=sub: e.tensor_copy(
                                vst[:, sub * 128:(sub + 1) * 128],
                                ps_tb[:, (sub % 2) * 1024:(sub % 2) * 1024 + 128]),
                                [(self.sMM, mtv)], self.sVE)
                            lnp_mark[("vtb", sub % 2)] = vev
                        lnp_mark[("vfm", tc % 2)] = mtv
                        vdma = self.sload(
                            "vd", 0, "sync",
                            self.vb.rearrange("(a sub r) (pp c) -> r a sub pp c",
                                              sub=4, r=128, pp=NP)[:, tc, :, p, :],
                            vst.rearrange("r (sub c) -> r sub c", sub=4),
                            [(self.sVE, vev)], n=1)
                        self.serial = _s3
                    lnp_mark[p] = self.sMM.v
                self.flush()


            self.serial = _sv or ("attn" in self.serial_parts)
            self.dma_drain()
            with contextlib.ExitStack() as sta:
                sba = lambda shape, dt, name: self.sbl(sta, shape, dt, name)
                ps_s = self.psum(sta, [128, 1024], F32, "ps_s")
                ps_pv = self.psum(sta, [128, 1024], F32, "ps_pv")
                ps_tf = self.psum(sta, [128, 1024], F32, "ps_tf")
                ps_sm = ps_tf
                ps_tb = self.psum(sta, [128, 2048], BF16, "ps_tb")
                vb_snap = self.dma_snapshot()
                with contextlib.ExitStack() as stn:
                    q2b = self.sbl(stn, [128, 2048], BF16, "q2b")
                    mrq = self.sbl(stn, [1, T], F32, "mrq")
                    last_mmn = None
                    vee = None
                    vemx = None
                    for half, fm in enumerate((k_fm, q_fm)):
                        for rnd in range(2):
                            if rnd * 4 >= NC:
                                continue
                            for p in range(NP):
                                wq2 = []
                                if half == 0 and rnd == 0 and p == 0:
                                    wq2.append((self.sVE, self.sVE.v))
                                if half == 1 and rnd == 0 and p == 0:
                                    wq2.append((self.sVE, vemx))
                                if last_mmn is not None:
                                    wq2.append((self.sMM, last_mmn))
                                if vee is not None and p == 0:
                                    wq2.append((self.sVE, vee))
                                rw_ = min(2048, T)
                                acq = self.op("scalar", lambda e, fm=fm, p=p, rnd=rnd, rw_=rw_: e.activation(
                                    q2b[:, 0:rw_],
                                    fm[:, p * T + rnd * 2048: p * T + rnd * 2048 + rw_],
                                    AF.Square), wq2, self.sAC)
                                for tci in range(4):
                                    tc = rnd * 4 + tci
                                    if tc >= NC:
                                        continue
                                    psn = ps_pv if tci < 2 else ps_s
                                    mmn = self.op("tensor", lambda e, tci=tci, p=p, psn=psn: e.matmul(
                                        psn[:, (tci % 2) * 512:(tci % 2 + 1) * 512][0:1, :],
                                        self.ones_col, q2b[:, tci * 512:(tci + 1) * 512],
                                        start=(p == 0), stop=(p == NP - 1),
                                        skip_group_check=True),
                                        [(self.sAC, acq)], self.sMM)
                                last_mmn = mmn
                            for tci in range(4):
                                tc = rnd * 4 + tci
                                if tc >= NC:
                                    continue
                                psn = ps_pv if tci < 2 else ps_s
                                vee = self.op("vector", lambda e, tci=tci, tc=tc, psn=psn: e.tensor_copy(
                                    mrq[0:1, tc * 512:(tc + 1) * 512],
                                    psn[0:1, (tci % 2) * 512:(tci % 2 + 1) * 512]),
                                    [(self.sMM, mmn)], self.sVE)
                        if half == 0:
                            vemx = self.op("vector", lambda e: e.reduce_max(
                                self.col[0:1, 7:8], mrq, axis=AX),
                                [(self.sVE, vee)], self.sVE)
                    vem2 = self.op("vector", lambda e: e.tensor_scalar(
                        mrq, mrq, self.col[0:1, 7:8], None, ALU.mult),
                        [(self.sVE, vee)], self.sVE)
                    self.op("scalar", lambda e: e.activation(mrq, mrq, AF.Ln),
                            [(self.sVE, vem2)])
                    acm = self.op("scalar", lambda e: e.activation(
                        mrq, mrq, AF.Exp, scale=0.5), inc=self.sAC)
                    vemr = self.op("vector", lambda e: e.tensor_scalar(
                        mrb, mrq, self.cst[0:1, 4:5], self.cst[0:1, 8:9],
                        ALU.mult, ALU.add), [(self.sAC, acm)], self.sVE)
                    if self.debug:
                        self.dma("gpsimd", self.dbg["d_qb"].rearrange("c p t -> p c t"),
                                 q_fm.rearrange("p (c t) -> p c t", c=NP), self.sGP,
                                 [(self.sVE, vemr)])
                        self.dma("gpsimd", self.dbg["d_kb"].rearrange("c p t -> p c t"),
                                 k_fm.rearrange("p (c t) -> p c t", c=NP), self.sGP)
                        self.op("gpsimd", lambda e, v=self.sST.v: e.wait_ge(self.sST.h, v))
                        self.dma("gpsimd", self.dbg["d_vb"], self.vb, self.sGP)
                        self.dma("gpsimd", self.dbg["d_mb"], mrb, self.sGP,
                                 [(self.sVE, vemr)])
                    self.flush()
                eT = sba([128, KT * 512], BF16, "eT")
                vbf = sba([128, 2 * 512], BF16, "vbf")
                of = sba([128, 8 * 512], F32, "of")
                srow = of[0:1, 0:512]
                rc4 = sba([128, 4], F32, "rc4")
                xa = sba([128, D], F32, "xa")
                xn = self.scr
                lnt = self.sq
                spl = self.scr2.bitcast(BF16)[:, 0:NP * 128]
                trf = MK.TrPool(self, ps_tf, 512, self.ident_f, nslots=2)
                trb = MK.TrPool(self, ps_tb, 1024, self.ident_b, nslots=2)
                spl_dma = None
                xn_marks = {}
                pv_evac = None
                sum_evac = None
                for qc in range(NC):
                    ktmax = 4 * qc + 4
                    exp_prev = None
                    mask_marks = {}
                    ldxs = {}
                    exp_slot = {}
                    for kt in range(ktmax):
                        sl_ = (kt % 2) * 512
                        wts = []
                        if exp_slot.get(kt % 2) is not None:
                            wts.append((self.sAC, exp_slot[kt % 2]))
                        if kt == 0 and pv_evac is not None:
                            wts.append((self.sAC, pv_evac))
                        for pp in range(NP):
                            mmq = self.op("tensor", lambda e, pp=pp, kt=kt, qc=qc, sl_=sl_: e.matmul(
                                ps_s[:, sl_:sl_ + 512],
                                k_fm[:, pp * T + kt * 128: pp * T + kt * 128 + 128],
                                q_fm[:, pp * T + qc * 512: pp * T + (qc + 1) * 512],
                                start=(pp == 0), stop=False, skip_group_check=True),
                                wts if pp == 0 else (), self.sMM)
                        mmq = self.op("tensor", lambda e, qc=qc, sl_=sl_: e.matmul(
                            ps_s[:, sl_:sl_ + 512], self.ones_row,
                            mrb[0:1, qc * 512:(qc + 1) * 512],
                            start=False, stop=True, skip_group_check=True),
                            [(self.sVE, vemr)], self.sMM)
                        exp_prev = self.op("scalar", lambda e, kt=kt, sl_=sl_: e.activation(
                            eT[:, kt * 512:(kt + 1) * 512], ps_s[:, sl_:sl_ + 512], AF.Exp),
                            [(self.sMM, mmq)], self.sAC)
                        exp_slot[kt % 2] = exp_prev
                        i_d = kt - 4 * qc
                        if i_d >= 0:
                            for sb4 in range(4):
                                delta = sb4 - i_d
                                slc = eT[:, kt * 512 + sb4 * 128: kt * 512 + (sb4 + 1) * 128]
                                if delta < 0:
                                    self.op("vector", lambda e, slc=slc: e.tensor_scalar(
                                        slc, slc, self.cst[:, 10:11], None, ALU.mult),
                                        [(self.sAC, exp_prev)], self.sVE)
                                elif delta == 0:
                                    self.op("vector", lambda e, slc=slc: e.tensor_tensor(
                                        slc, slc, self.causal_b, ALU.mult),
                                        [(self.sAC, exp_prev)], self.sVE)
                            mask_marks[kt] = self.sVE.v
                    for kt in range(ktmax):
                        wts = [(self.sAC, exp_prev), (self.sVE, mask_marks.get(kt))]
                        if kt == 0:
                            if trf.ev_marks and trf.ev_marks[-1] is not None:
                                wts.append(trf.ev_marks[-1])
                        mms = self.op("tensor", lambda e, kt=kt: e.matmul(
                            ps_sm[0:1, 0:512], self.ones_col,
                            eT[:, kt * 512:(kt + 1) * 512],
                            start=(kt == 0), stop=(kt == ktmax - 1),
                            skip_group_check=True), wts, self.sMM)
                    sum_evac = self.op("vector", lambda e: e.tensor_copy(
                        srow, ps_sm[0:1, 0:512]), [(self.sMM, mms)], self.sVE)
                    for i in range(4):
                        dst_ps, mm, idx = trf.tr(srow[0:1, i * 128:(i + 1) * 128],
                                                 [(self.sVE, sum_evac)])
                        ver = self.op("vector", lambda e, d=dst_ps, i=i: e.reciprocal(
                            rc4[:, i:i + 1], d), [(self.sMM, mm)], self.sVE)
                        trf.done(idx, (self.sVE, ver))
                    for qtr in range(4):
                        vload = None
                        for kt in range(ktmax):
                            bv = kt % 2
                            wts = list(vb_snap) if (qc == 0 and qtr == 0 and kt == 0) else []
                            pm = mask_marks.get(("vbl", bv))
                            if pm is not None:
                                wts.append((self.sMM, pm))
                            vload = self.sload("vbf", kt, "sync",
                                               vbf[:, bv * 256:(bv + 1) * 256],
                                               self.vb[kt * 128:(kt + 1) * 128,
                                                       qtr * 256:(qtr + 1) * 256], wts)
                            for dcq in range(2):
                                wts2 = [vload, (self.sVE, mask_marks.get(kt) if isinstance(mask_marks.get(kt), int) else None)]
                                if kt == 0 and pv_evac is not None:
                                    wts2.append((self.sAC, pv_evac))
                                mmp = self.op("tensor", lambda e, kt=kt, dcq=dcq, bv=bv: e.matmul(
                                    ps_pv[:, dcq * 512:(dcq + 1) * 512],
                                    vbf[:, bv * 256 + dcq * 128: bv * 256 + (dcq + 1) * 128],
                                    eT[:, kt * 512:(kt + 1) * 512],
                                    start=(kt == 0), stop=(kt == ktmax - 1),
                                    skip_group_check=True),
                                    wts2 if dcq == 0 else (), self.sMM)
                            mask_marks[("vbl", bv)] = mmp
                        for dcq in range(2):
                            pv_evac = self.op("scalar", lambda e, qtr=qtr, dcq=dcq: e.activation(
                                of[:, (qtr * 2 + dcq) * 512:(qtr * 2 + dcq + 1) * 512],
                                ps_pv[:, dcq * 512:(dcq + 1) * 512], AF.Identity),
                                [(self.sMM, mmp)], self.sAC)
                    for ts_ in range(4):
                        tt = qc * 4 + ts_
                        wts = [(self.sVE, xn_marks.get(tt - 1))]
                        ldx = self.sload("xa1", tt, "sync", xa,
                                         self.xres[tt * 128:(tt + 1) * 128, :], wts)
                        for oc in range(NP):
                            dst_ps, mm, idx = trf.tr(
                                of[:, oc * 512 + ts_ * 128: oc * 512 + (ts_ + 1) * 128],
                                [(self.sAC, pv_evac)])
                            out_ap = xn.rearrange("p (di h) -> p h di", h=NH)[:, 2 * oc:2 * oc + 2, :]
                            in_ap = xa.rearrange("p (di h) -> p h di", h=NH)[:, 2 * oc:2 * oc + 2, :]
                            wxn = [(self.sMM, mm), ldx]
                            if oc == 0 and self._xst.get(tt - 1) is not None:
                                wxn.append(self._xst[tt - 1])
                            vex = self.op("vector", lambda e, d=dst_ps, o=out_ap, i_=in_ap,
                                          ts_=ts_: e.scalar_tensor_tensor(
                                o, d.rearrange("p (hh di) -> p hh di", hh=2),
                                rc4[:, ts_:ts_ + 1], i_, ALU.mult, ALU.add),
                                wxn, self.sVE)
                            trf.done(idx, (self.sVE, vex))
                        self._xst[tt] = self.sload("xst", tt, "sync",
                                                   self.xres[tt * 128:(tt + 1) * 128, :],
                                                   xn, [(self.sVE, vex)])
                        ve_ln = self.ln_tile(xn, lnt, extra_src_waits=[(self.sVE, vex)])
                        xn_marks[tt] = ve_ln
                        for dc in range(NP):
                            wts2 = [(self.sVE, ve_ln)]
                            if dc == 0 and spl_dma is not None:
                                wts2.append(spl_dma)
                            dst_ps, mm, idx = trb.tr(lnt[:, dc * 128:(dc + 1) * 128], wts2)
                            ev2 = self.op("vector", lambda e, d=dst_ps, dc=dc: e.tensor_copy(
                                spl[:, dc * 128:(dc + 1) * 128], d), [(self.sMM, mm)], self.sVE)
                            trb.done(idx, (self.sVE, ev2))
                        spl_dma = self.sload(
                            "spl", tt, "sync",
                            self.lnfm.rearrange("p d t -> d p t")[:, :, tt * 128:(tt + 1) * 128],
                            spl.rearrange("d (p t) -> d p t", p=NP), [(self.sVE, ev2)])
                self.flush()
                if self.debug:
                    self.op("gpsimd", lambda e, v=self.sST.v: e.wait_ge(self.sST.h, v))
                    self.dma("gpsimd", self.dbg["d_x3"], self.xres, self.sGP)
                    self.dma("gpsimd", self.dbg["d_lnfm"], self.lnfm, self.sGP)
                    self.flush()
            self.serial = _sv

    def finalize(self):
        self.dma_drain(engines=("sync",))
        self.flush()


def _rope_tables(T):
    half = D // 2
    inv = 1.0 / (ROPE_BASE ** (np.arange(half, dtype=np.float64) / half))
    ang = np.arange(T, dtype=np.float64)[:, None] * inv[None, :]
    cos, sin = np.cos(ang), np.sin(ang)
    cosP = np.zeros((NP, HD, T), np.float32)
    sinP = np.zeros((NP, HD, T), np.float32)
    for p in range(NP):
        for r in range(HD):
            i = 8 * r + p
            cosP[p, r, :] = cos[:, i]
            sinP[p, r, :] = sin[:, i]
    return cosP, sinP


def make_inputs(full, T):
    causal = np.triu(np.ones((128, 128), np.float32))
    cosP, sinP = _rope_tables(T)
    m = dict(
        Wq=full["L0_attn_a_Wq"], Wk=full["L0_attn_a_Wk"], Wv=full["L0_attn_a_Wv"],
        Wo=full["L0_attn_a_Wo"], qg=full["L0_attn_a_q_gamma"], qb=full["L0_attn_a_q_beta"],
        kg=full["L0_attn_a_k_gamma"], kb=full["L0_attn_a_k_beta"],
        wq3=np.ascontiguousarray(np.transpose(full["L0_attn_b_w_q"], (2, 0, 1))),
        wk3=np.ascontiguousarray(np.transpose(full["L0_attn_b_w_k"], (2, 0, 1))),
        wv3=np.ascontiguousarray(np.transpose(full["L0_attn_b_w_v"], (2, 0, 1))),
        rw=full["L0_ffn_router_w"], sg=full["L0_ffn_shared_gate_w"],
        su=full["L0_ffn_shared_up_w"], sd=full["L0_ffn_shared_down_w"],
        eg=full["L0_ffn_expert_gate_k"], eu=full["L0_ffn_expert_up_k"],
        ed=full["L0_ffn_expert_down_k"],
        identf=np.eye(128, dtype=np.float32), causalf=causal,
        cosP=cosP, sinP=sinP,
    )
    return {k: np.ascontiguousarray(np.asarray(v, dtype=np.float32)) for k, v in m.items()}


def kernel(**inputs):
    T = 4096
    mk = MK(T=T)
    nc = mk.build()
    base = make_inputs(inputs, T)
    x = np.ascontiguousarray(np.asarray(inputs["L0_attn_a_x"], dtype=np.float32))
    in_maps = [dict(base, x=x[c]) for c in range(8)]
    res = run_bass_kernel_spmd(nc, in_maps, core_ids=list(range(8)))
    return np.stack([r["y"] for r in res.results], axis=0)


# version 06
# speedup vs baseline: 1.0424x; 1.0424x over previous
import math
import contextlib
import numpy as np
import concourse.bass as bass
import concourse.mybir as mybir
from concourse.bass_utils import run_bass_kernel_spmd

F32 = mybir.dt.float32
BF16 = mybir.dt.bfloat16
AF = mybir.ActivationFunctionType
ALU = mybir.AluOpType
AX = mybir.AxisListType.X

D = 1024
NH = 16
HD = 64
NP = 8
IFF = 512
NEXP = 64
LN_EPS = 1e-5
CA = 45.0
CB = 10.0
ROPE_BASE = 10000.0
RINV = 1.0 / D


class Sem:

    def __init__(self, ctx, nc, name):
        self.h = ctx.enter_context(nc.semaphore(name))
        self.v = 0
        self.base = 0
        self.name = name


ENGINES = ("sync", "gpsimd", "tensor", "vector", "scalar")


class MK:
    def __init__(self, T=4096, phases=(1, 2, 3, 4), n_exp=NEXP, debug=False, serial=False,
                 serial_parts=()):
        self.serial = serial
        self.serial_parts = set(serial_parts)
        self.T = T
        self.NT = T // 128
        self.NC = T // 512
        self.SLAB = min(1024, T)
        self.NSLAB = T // self.SLAB
        self.ST = self.SLAB // 128
        self.SC = self.SLAB // 512
        self.phases = phases
        self.n_exp = n_exp
        self.debug = debug
        self.nc = bass.Bass("TRN2", target_bir_lowering=False)
        self.ctx = contextlib.ExitStack()
        self.ops = {e: [] for e in ENGINES}
        self.marks = {e: {} for e in ENGINES}
        self._xst = {}

    _CHAIN = {"vector": "sVE", "scalar": "sAC", "gpsimd": "sGPE"}

    def op(self, eng, fn, waits=(), inc=None, n=1):
        chain = self._CHAIN.get(eng)
        if chain is not None and not self.serial:
            cs = getattr(self, chain)
            waits = list(waits) + [(cs, cs.v)]
            if inc is None:
                inc = cs
        if self.serial:
            defsem = {"vector": "sVE", "scalar": "sAC", "tensor": "sMM",
                      "gpsimd": "sGPE"}.get(eng)
            if inc is None and defsem is not None:
                inc = getattr(self, defsem)
            waits = [(getattr(self, "s" + nm), getattr(self, "s" + nm).v)
                     for nm in ("LDX", "LDW", "LDA", "ST", "MM", "VE", "AC", "GP", "GPE")]
        wl = []
        wm = self.marks[eng]
        for s, v in waits:
            if v is None or v <= 0:
                continue
            if wm.get(s.name, 0) >= v:
                continue
            wm[s.name] = v
            pv = v - s.base
            if pv > 0:
                wl.append((s.h, pv))
        if inc is not None:
            inc.v += n
            self.ops[eng].append((fn, wl, inc.h, n))
            return inc.v
        self.ops[eng].append((fn, wl, None, 0))
        return None

    def flush(self):
        nc = self.nc
        queues = {e: self.ops[e] for e in ENGINES if self.ops[e]}
        if not queues:
            return
        self.ops = {e: [] for e in ENGINES}
        with nc.Block() as blk:
            def mk_runner(lst):
                def run(eng):
                    for fn, wl, inch, n in lst:
                        for h, v in wl:
                            eng.wait_ge(h, v)
                        instr = fn(eng)
                        if inch is not None:
                            instr.then_inc(inch, n)
                return run
            for e, lst in queues.items():
                getattr(blk, e)(mk_runner(lst))

    _uid = 0

    def _nm(self, name):
        MK._uid += 1
        return f"{name}_{MK._uid}"

    def sb(self, shape, dt, name):
        return self.ctx.enter_context(self.nc.sbuf_tensor(self._nm(name), shape, dt))[:]

    def sbl(self, stack, shape, dt, name):
        return stack.enter_context(self.nc.sbuf_tensor(self._nm(name), shape, dt))[:]

    def psum(self, stack, shape, dt, name):
        return stack.enter_context(self.nc.psum_tensor(self._nm(name), shape, dt))[:]

    def dma(self, eng, dst, src, sem, waits=()):
        self.op(eng, lambda e, d=dst, s=src: e.dma_start(d, s), waits, sem, 16)
        return sem.v

    _xst = {}

    def dma_snapshot(self):
        out = [(s, s.v) for s in (self.sLDX, self.sLDA, self.sST, self.sGP, self.sLDW)]
        for sems in getattr(self, "_streams", {}).values():
            out.extend((s, s.v) for s in sems)
        return [w for w in out if w[1] > 0]

    def dma_drain(self, engines=("sync", "gpsimd", "tensor", "vector", "scalar")):
        snap = self.dma_snapshot()
        if not snap:
            return
        for eng in engines:
            s0, v0 = snap[-1]
            pv0 = max(v0 - s0.base, 0)
            self.op(eng, lambda e, s0=s0, pv0=pv0: e.wait_ge(s0.h, pv0), snap[:-1])

    def ds(self, name, n=2):
        if not hasattr(self, "_streams"):
            self._streams = {}
        if name not in self._streams:
            self._streams[name] = [Sem(self.ctx, self.nc, f"{name}{i}") for i in range(n)]
        return self._streams[name]

    def sload(self, name, idx, eng, dst, src, waits=(), n=2):
        sems = self.ds(name, n)
        s = sems[idx % n]
        self.op(eng, lambda e, d=dst, sc=src: e.dma_start(d, sc), waits, s, 16)
        return (s, s.v)

    def declare_io(self):
        nc, T = self.nc, self.T
        d = lambda n, s, k="ExternalInput": nc.dram_tensor(n, s, F32, kind=k).ap()
        self.xin = d("x", [T, D])
        self.Wq, self.Wk, self.Wv, self.Wo = (d(n, [D, D]) for n in ("Wq", "Wk", "Wv", "Wo"))
        self.qg, self.qb, self.kg, self.kb = (d(n, [D]) for n in ("qg", "qb", "kg", "kb"))
        self.wq3, self.wk3, self.wv3 = (d(n, [NH, HD, HD]) for n in ("wq3", "wk3", "wv3"))
        self.rw = d("rw", [D, NEXP])
        self.sg, self.su, self.sd = d("sg", [D, IFF]), d("su", [D, IFF]), d("sd", [IFF, D])
        self.eg, self.eu = d("eg", [NEXP, D, IFF]), d("eu", [NEXP, D, IFF])
        self.ed = d("ed", [NEXP, IFF, D])
        self.identf = d("identf", [128, 128])
        self.cosP = d("cosP", [NP, HD, T])
        self.sinP = d("sinP", [NP, HD, T])
        self.causalf = d("causalf", [128, 128])
        self.yout = d("y", [T, D], "ExternalOutput")
        di = lambda n, s, dt=BF16: nc.dram_tensor(n, s, dt, kind="Internal").ap()
        self.xres = di("xres", [T, D], F32)
        self.qs = di("qs", [NP, 128, T])
        self.ks = di("ks", [NP, 128, T])
        self.vs = di("vs", [NP, T, 128])
        self.afm = di("afm", [NP, 128, T])
        self.lnfm = di("lnfm", [NP, 128, T])
        self.lnpm = di("lnpm", [NP, 128, T])
        self.vb = di("vb", [T, D])
        self.wfme = di("wfme", [NEXP, T], F32)
        self.dbg = {}
        if self.debug:
            for nm, shp in [("d_q", [NP, 128, T]), ("d_k", [NP, 128, T]),
                            ("d_vs", [NP, T, 128]), ("d_m", [NH, T]),
                            ("d_afm", [NP, 128, T]), ("d_x1", [T, D]),
                            ("d_lnfm", [NP, 128, T]), ("d_lnpm", [NP, 128, T]),
                            ("d_w", [NEXP, T]),
                            ("d_qb", [NP, 128, T]), ("d_kb", [NP, 128, T]),
                            ("d_vb", [T, D]), ("d_mb", [1, T]), ("d_x3", [T, D]),
                            ("d_ln1", [T, D]), ("d_gb", [128, 4 * D]), ("d_col", [128, 16]),
                            ("d_et", [128, 1024]), ("d_pvf", [65, 512]), ("d_ms", [1, 1024]),
                            ("d_idb", [128, 128]), ("d_qln", [T, D])]:
                self.dbg[nm] = nc.dram_tensor(nm, shp, F32, kind="ExternalOutput").ap()

    def alloc_global(self):
        sb = self.sb
        self.ident_f = sb([128, 128], F32, "ident_f")
        self.ident_b = sb([128, 128], BF16, "ident_b")
        self.causal_b = sb([128, 128], BF16, "causal_b")
        self.ones_col = sb([128, 1], BF16, "ones_col")
        self.ones_row = sb([1, 128], BF16, "ones_row")
        self.onesf_row = sb([1, 128], F32, "onesf_row")
        self.col = sb([128, 16], F32, "col")
        self.cstv = [LN_EPS, RINV, -0.5, 0.5, -1.0, 1e-9, 2.0, CA, CB, 1.0 / 32.0, 0.0, 1.0, -1e30]
        self.cst = sb([128, len(self.cstv)], F32, "cst")
        for i, v in enumerate(self.cstv):
            self.nc.const_aps.register(F32, v, self.cst[:, i:i + 1])
        self.scr = sb([128, D], F32, "scr")
        self.scr2 = sb([128, D], F32, "scr2")
        self.dump = sb([128, 512], F32, "dump")
        self.sq = sb([128, D], BF16, "sq")
        for name in ("LDX", "LDW", "LDA", "ST", "MM", "VE", "AC", "GP", "GPE"):
            setattr(self, "s" + name, Sem(self.ctx, self.nc, name))
        self._ln_prev = None
        self._qk_prev = None

    def act_load(self, funcs):
        from concourse.hw_specs import get_activation_tables
        tabs = get_activation_tables(self.nc.m.arch)
        for idx, fl in enumerate(tabs.values()):
            if all(f in fl for f in funcs):
                def emit(e, idx=idx):
                    return e.add_instruction(mybir.InstLoadActFuncSet(
                        name=self.nc.get_next_instruction_name(),
                        act_func_set_id=idx, ins=[], outs=[]))
                self.op("scalar", emit, inc=self.sAC)
                return
        raise ValueError(f"no table set contains {funcs}")

    def p0(self):
        self.dma("sync", self.ident_f, self.identf, self.sLDX)
        self.dma("gpsimd", self.ident_b, self.identf, self.sGP)
        self.dma("gpsimd", self.causal_b, self.causalf, self.sGP)
        for t in (self.ones_col, self.ones_row, self.onesf_row):
            self.op("gpsimd", lambda e, t=t: e.memset(t, 1.0))
        for i, v in enumerate(self.cstv):
            self.op("gpsimd", lambda e, i=i, v=v: e.memset(self.cst[:, i:i + 1], v),
                    inc=self.sGPE)
        self.act_load([AF.Exp, AF.Ln, AF.Square, AF.Identity])
        lv, gv, gev = self.sLDX.v, self.sGP.v, self.sGPE.v
        self.op("sync", lambda e: e.wait_ge(self.sLDX.h, lv))
        self.op("gpsimd", lambda e: e.wait_ge(self.sGP.h, gv))
        self.op("gpsimd", lambda e: e.wait_ge(self.sGPE.h, gev))
        self.flush()

    def ln_tile(self, src, out_bf, extra_src_waits=()):
        c = self.col
        s_, m_, t_, ss_, e2_, v_, r_ = (c[:, i:i + 1] for i in range(7))
        w = list(extra_src_waits)
        self.op("vector", lambda e: e.reduce_sum(s_, src, axis=AX), w)
        ve_mu = self.op("vector",
                        lambda e: e.tensor_scalar(m_, s_, self.cst[:, 1:2], None, ALU.mult),
                        inc=self.sVE)
        wsq = [(self.sVE, ve_mu)] + ([(self.sVE, self._ln_prev)] if self._ln_prev else [])
        self.op("scalar", lambda e: e.activation(t_, m_, AF.Square), wsq)
        h2_ = c[:, 7:8]
        self.op("scalar", lambda e: e.activation(self.dump, src[:, 0:512], AF.Square,
                                                 accum_out=ss_), w)
        ac_h = self.op("scalar", lambda e: e.activation(self.dump, src[:, 512:1024],
                                                        AF.Square, accum_out=h2_),
                       inc=self.sAC)
        ve_h = self.op("vector", lambda e: e.tensor_scalar(ss_, ss_, h2_, None, ALU.add),
                       [(self.sAC, ac_h)], self.sVE)
        ac_ss = self.op("scalar",
                        lambda e: e.activation(e2_, ss_, AF.Identity, scale=RINV),
                        [(self.sVE, ve_h)], self.sAC)
        ve_var = self.op("vector", lambda e: e.tensor_scalar(v_, e2_, t_, None, ALU.subtract),
                         [(self.sAC, ac_ss)], self.sVE)
        self._ln_prev = ve_var
        self.op("scalar", lambda e: e.activation(r_, v_, AF.Ln, bias=LN_EPS),
                [(self.sVE, ve_var)])
        ac_r = self.op("scalar", lambda e: e.activation(r_, r_, AF.Exp, scale=-0.5),
                       inc=self.sAC)
        ve_out = self.op("vector",
                         lambda e: e.tensor_scalar(out_bf, src, m_, r_, ALU.subtract, ALU.mult),
                         [(self.sAC, ac_r)], self.sVE)
        return ve_out

    class TrPool:

        def __init__(self, mk, ps_ap, slot_stride, ident, nslots=2):
            self.mk, self.ps, self.stride, self.ident, self.n = mk, ps_ap, slot_stride, ident, nslots
            self.tr_marks = []
            self.ev_marks = []

        def tr(self, src, in_waits=()):
            i = len(self.tr_marks)
            w = list(in_waits)
            if i >= self.n:
                w.append(self.ev_marks[i - self.n])
            kk, mm_ = src.shape[0], src.shape[1]
            base = (i % self.n) * self.stride
            dst = self.ps[0:mm_, base:base + kk]
            mm = self.mk.op("tensor",
                            lambda e, d=dst, s=src, kk=kk: e.transpose(
                                d, s, self.ident[0:kk, 0:kk]),
                            w, self.mk.sMM)
            self.tr_marks.append(mm)
            self.ev_marks.append(None)
            return dst, mm, i

        def done(self, i, mark):
            self.ev_marks[i] = mark

    def p1b(self):
        nc, T, NT = self.nc, self.T, self.NT
        st = contextlib.ExitStack()
        sb = lambda shape, dt, name: self.sbl(st, shape, dt, name)
        with st:
            stp = contextlib.ExitStack()
            ps_tr = self.psum(stp, [128, 2048], BF16, "ps_tr")
            ps_q = self.psum(stp, [128, 1024], F32, "ps_q")
            ps_k = self.psum(stp, [128, 1024], F32, "ps_k")
            ps_v = self.psum(stp, [128, 1024], F32, "ps_v")
            wq_sb = sb([128, NP * D], BF16, "wq_sb")
            wk_sb = sb([128, NP * D], BF16, "wk_sb")
            wv_sb = sb([128, NP * D], BF16, "wv_sb")
            gb = sb([128, 4 * D], F32, "gb")
            gbrow = sb([1, D], F32, "gbrow")
            xa = sb([128, 2 * D], F32, "xa")
            lnt = sb([128, 2 * D], BF16, "lnt")
            lnf = sb([128, 2 * NP * 128], BF16, "lnf")
            qkv = sb([128, 3 * D], BF16, "qkv")
            spq = sb([128, NP * 128], BF16, "spq")
            spk = sb([128, NP * 128], BF16, "spk")
            qn2 = sb([128, NT * 16], F32, "qn2")
            kn2 = sb([128, NT * 16], F32, "kn2")


            for w, dst in ((self.Wq, wq_sb), (self.Wk, wk_sb), (self.Wv, wv_sb)):
                self.dma("gpsimd", dst.rearrange("p (dc o) -> p dc o", dc=NP),
                         w.rearrange("(dc p) o -> p dc o", p=128), self.sGP)
            ldw = self.sGP.v
            ev_half = {}
            for i, w in enumerate((self.qg, self.qb, self.kg, self.kb)):
                ldg = self.dma("sync", gbrow, w[None, :], self.sLDX,
                               [(self.sMM, ev_half.get("mm"))])
                for half in range(2):
                    sl = slice(i * D + half * 512, i * D + (half + 1) * 512)
                    wts = [(self.sLDX, ldg)]
                    if ev_half.get(half) is not None:
                        wts.append((self.sVE, ev_half[half]))
                    mm = self.op("tensor",
                                 lambda e, sl=sl, h=half: e.matmul(
                                     ps_q[:, h * 512:(h + 1) * 512], self.onesf_row,
                                     gbrow[0:1, h * 512:(h + 1) * 512], start=True,
                                     stop=True, skip_group_check=True),
                                 wts, self.sMM)
                    ev_half[half] = self.op("vector",
                                            lambda e, sl=sl, h=half: e.tensor_copy(
                                                gb[:, sl], ps_q[:, h * 512:(h + 1) * 512]),
                                            [(self.sMM, mm)], self.sVE)
                ev_half["mm"] = mm
            self.flush()

            trp = MK.TrPool(self, ps_tr, 1024, self.ident_b)
            xfree = {}
            lnt_free = {}
            pjfree = {"q": None, "k": None, "v": None}
            spfree = {"q": None, "k": None}
            vspill = None
            qkvfree = {"q": None, "k": None, "v": None}
            for tt in range(NT):
                b = tt % 2
                xslice = xa[:, b * D:(b + 1) * D]
                w = [(self.sVE, xfree.get(tt - 2))] if tt >= 2 else []
                ldx = self.sload("xa1", tt, "sync", xslice,
                                 self.xin[tt * 128:(tt + 1) * 128, :], w)
                lslice = lnt[:, b * D:(b + 1) * D]
                lfb = lnf[:, b * NP * 128:(b + 1) * NP * 128]
                ve_ln = self.ln_tile(xslice, lslice,
                                     extra_src_waits=[ldx,
                                                      (self.sMM, lnt_free.get(tt - 2))])
                xfree[tt] = ve_ln
                if self.debug and tt == 0:
                    self.dma("gpsimd", self.dbg["d_col"], self.col, self.sGP,
                             [(self.sVE, ve_ln)])
                if self.debug:
                    self.dma("gpsimd", self.dbg["d_ln1"][tt * 128:(tt + 1) * 128, :],
                             lslice, self.sGP, [(self.sVE, ve_ln)])
                prev_proj = pjfree.get("lnf" + str(b))
                for dc in range(NP):
                    dst_ps, mm, i = trp.tr(lslice[:, dc * 128:(dc + 1) * 128],
                                           [(self.sVE, ve_ln)])
                    ev = self.op("vector",
                                 lambda e, d=dst_ps, dc=dc, lfb=lfb: e.tensor_copy(
                                     lfb[:, dc * 128:(dc + 1) * 128], d[:, 0:128]),
                                 [(self.sMM, mm)] + ([(self.sMM, prev_proj)] if prev_proj else []),
                                 self.sVE)
                    trp.done(i, (self.sVE, ev))
                lnt_free[tt] = trp.tr_marks[-1]
                lnf_ready = ev
                pj = {}
                for pi, (wsb, psd, nm) in enumerate(
                        ((wq_sb, ps_q, "q"), (wk_sb, ps_k, "k"), (wv_sb, ps_v, "v"))):
                    wts = [(self.sVE, lnf_ready), (self.sGP, ldw)]
                    if pjfree[nm] is not None:
                        wts.append(pjfree[nm])
                    for half in range(2):
                        for dc in range(NP):
                            mm = self.op("tensor",
                                         lambda e, psd=psd, h=half, dc=dc, wsb=wsb, lfb=lfb: e.matmul(
                                             psd[:, h * 512:(h + 1) * 512],
                                             lfb[:, dc * 128:(dc + 1) * 128],
                                             wsb[:, dc * D + h * 512: dc * D + (h + 1) * 512],
                                             start=(dc == 0), stop=(dc == NP - 1),
                                             skip_group_check=True),
                                         wts if (half == 0 and dc == 0) else (), self.sMM)
                    pj[nm] = mm
                pjfree["lnf" + str(b)] = pj["v"]
                for si, (psd, nm, goff) in enumerate(((ps_q, "q", 0), (ps_k, "k", 2 * D))):
                    c = self.col
                    s_, m_, t_, ss_, e2_, v_, r_ = (c[:, 8 + i:9 + i] for i in range(7))
                    ve1 = self.op("vector", lambda e, p=psd: e.reduce_sum(s_, p, axis=AX),
                                  [(self.sMM, pj[nm])], self.sVE)
                    ve_mu = self.op("vector", lambda e: e.tensor_scalar(
                        m_, s_, self.cst[:, 1:2], None, ALU.mult), inc=self.sVE)
                    wq1 = [(self.sVE, ve_mu), (self.sVE, self._qk_prev)] if self._qk_prev \
                        else [(self.sVE, ve_mu)]
                    self.op("scalar", lambda e: e.activation(t_, m_, AF.Square), wq1)
                    h2_ = self.col[:, 7:8]
                    self.op("scalar", lambda e, p=psd: e.activation(
                        self.dump, p[:, 0:512], AF.Square, accum_out=ss_),
                        [(self.sVE, ve1)])
                    ach = self.op("scalar", lambda e, p=psd: e.activation(
                        self.dump, p[:, 512:1024], AF.Square, accum_out=h2_),
                        inc=self.sAC)
                    veh = self.op("vector", lambda e: e.tensor_scalar(
                        ss_, ss_, h2_, None, ALU.add), [(self.sAC, ach)], self.sVE)
                    ac1 = self.op("scalar", lambda e: e.activation(
                        e2_, ss_, AF.Identity, scale=RINV), [(self.sVE, veh)], self.sAC)
                    vev = self.op("vector", lambda e: e.tensor_scalar(
                        v_, e2_, t_, None, ALU.subtract), [(self.sAC, ac1)], self.sVE)
                    self._qk_prev = vev
                    self.op("scalar", lambda e: e.activation(r_, v_, AF.Ln, bias=LN_EPS),
                            [(self.sVE, vev)])
                    ac2 = self.op("scalar", lambda e: e.activation(r_, r_, AF.Exp, scale=-0.5),
                                  inc=self.sAC)
                    wts = [(self.sAC, ac2)]
                    if qkvfree[nm] is not None:
                        wts.append(qkvfree[nm])
                    self.op("vector", lambda e, p=psd: e.tensor_scalar(
                        self.scr, p, m_, r_, ALU.subtract, ALU.mult), wts)
                    self.op("vector", lambda e, g=goff: e.tensor_tensor(
                        self.scr2, self.scr, gb[:, g:g + D], ALU.mult))
                    ve_ln2 = self.op("vector", lambda e, g=goff, nm=nm: e.tensor_tensor(
                        qkv[:, (0 if nm == "q" else D):(D if nm == "q" else 2 * D)],
                        self.scr2, gb[:, g + D:g + 2 * D], ALU.add), inc=self.sVE)
                    pjfree[nm] = (self.sVE, ve_ln2)
                    if self.debug and nm == "q":
                        self.dma("gpsimd", self.dbg["d_qln"][tt * 128:(tt + 1) * 128, :],
                                 qkv[:, 0:D], self.sGP, [(self.sVE, ve_ln2)])
                    acsq = self.op("scalar", lambda e, nm=nm: e.activation(
                        self.sq,
                        qkv[:, (0 if nm == "q" else D):(D if nm == "q" else 2 * D)],
                        AF.Square), [(self.sVE, ve_ln2)], self.sAC)
                    n2 = qn2 if nm == "q" else kn2
                    for h in range(NH):
                        ven = self.op("vector", lambda e, h=h, n2=n2, tt=tt: e.reduce_sum(
                            n2[:, tt * 16 + h:tt * 16 + h + 1],
                            self.sq[:, h * 64:(h + 1) * 64], axis=AX),
                            [(self.sAC, acsq)] if h == 0 else (), self.sVE)
                wts = [(self.sMM, pj["v"])]
                if qkvfree["v"] is not None:
                    wts.append(qkvfree["v"])
                self.op("vector", lambda e: e.tensor_copy(qkv[:, 2 * D:2 * D + 512],
                                                          ps_v[:, 0:512]), wts)
                ve_v = self.op("vector", lambda e: e.tensor_copy(qkv[:, 2 * D + 512:3 * D],
                                                                 ps_v[:, 512:1024]),
                               inc=self.sVE)
                pjfree["v"] = (self.sVE, ve_v)
                for si, (nm, stg, sdram) in enumerate((("q", spq, self.qs), ("k", spk, self.ks))):
                    off = 0 if nm == "q" else D
                    for dc in range(NP):
                        wts = []
                        if dc == 0:
                            wts.append(pjfree[nm])
                            if spfree[nm] is not None:
                                wts.append(spfree[nm])
                        dst_ps, mm, i = trp.tr(qkv[:, off + dc * 128: off + (dc + 1) * 128], wts)
                        ev = self.op("vector", lambda e, d=dst_ps, dc=dc, stg=stg: e.tensor_copy(
                            stg[:, dc * 128:(dc + 1) * 128], d[:, 0:128]),
                            [(self.sMM, mm)], self.sVE)
                        trp.done(i, (self.sVE, ev))
                    qkvfree[nm] = (self.sMM, mm)
                    spfree[nm] = self.sload(
                        "sp" + nm, tt, "sync",
                        sdram.rearrange("p d t -> d p t")[:, :, tt * 128:(tt + 1) * 128],
                        stg.rearrange("d (p t) -> d p t", p=NP), [(self.sVE, ev)])
                wts = [(self.sVE, ve_v)]
                if vspill is not None:
                    wts.append(vspill)
                vspill = self.sload(
                    "vsp", tt, "sync",
                    self.vs.rearrange("p t d -> t p d")[tt * 128:(tt + 1) * 128],
                    qkv[:, 2 * D:3 * D].rearrange("t (p d) -> t p d", p=NP), wts)
                qkvfree["v"] = vspill
            self.flush()

            stp.close()
            st2 = contextlib.ExitStack()
            ps_trf = self.psum(st2, [128, 1024], F32, "ps_trf")
            trf = MK.TrPool(self, ps_trf, 512, self.ident_f)
            for si, (n2, base) in enumerate(((qn2, 0), (kn2, T))):
                for tt in range(NT):
                    dst_ps, mm, i = trf.tr(n2[:, tt * 16:(tt + 1) * 16])
                    ev = self.op("vector", lambda e, d=dst_ps, tt=tt, base=base: e.tensor_copy(
                        self.nrm[0:16, base + tt * 128: base + (tt + 1) * 128],
                        d[0:16, 0:128]), [(self.sMM, mm)], self.sVE)
                    trf.done(i, (self.sVE, ev))
            mx = self.col[0:16, 14:15]
            ve_mx = self.op("vector", lambda e: e.reduce_max(
                mx, self.nrm[0:16, T:2 * T], axis=AX), [(self.sVE, ev)], self.sVE)
            ve_m2 = self.op("vector", lambda e: e.tensor_scalar(
                self.nrm[0:16, T:2 * T], self.nrm[0:16, 0:T], mx, None, ALU.mult),
                [(self.sVE, ve_mx)], self.sVE)
            self.op("scalar", lambda e: e.activation(
                self.nrm[0:16, T:2 * T], self.nrm[0:16, T:2 * T], AF.Ln),
                [(self.sVE, ve_m2)])
            ac_m = self.op("scalar", lambda e: e.activation(
                self.nrm[0:16, T:2 * T], self.nrm[0:16, T:2 * T], AF.Exp, scale=0.5),
                inc=self.sAC)
            ve_mr = self.op("vector", lambda e: e.tensor_scalar(
                self.mrow, self.nrm[0:16, T:2 * T], self.cst[0:16, 4:5],
                self.cst[0:16, 7:8], ALU.mult, ALU.add),
                [(self.sAC, ac_m)], self.sVE)
            if self.debug:
                self.dma("gpsimd", self.dbg["d_m"], self.mrow, self.sGP,
                         [(self.sVE, ve_mr)])
                self.dma("gpsimd", self.dbg["d_gb"], gb, self.sGP)
                self.dma("gpsimd", self.dbg["d_idb"], self.ident_b, self.sGP)
            self.flush()
            st2.close()

    def p1c(self):
        nc, T, NC = self.nc, self.T, self.NC
        KT = T // 128
        st = contextlib.ExitStack()
        sb = lambda shape, dt, name: self.sbl(st, shape, dt, name)
        with st:
            ps_s0 = self.psum(st, [128, 1024], F32, "ps_s0")
            ps_s1 = self.psum(st, [128, 1024], F32, "ps_s1")
            ps_pv = self.psum(st, [128, 512], F32, "ps_pv")
            ps_nf = self.psum(st, [128, 1024], F32, "ps_nf")
            ps_bt = self.psum(st, [128, 128], BF16, "ps_bt")
            qp = sb([128, 4 * T], BF16, "qp")
            kp = sb([128, 4 * T], BF16, "kp")
            vp = sb([128, 2 * KT * 130], BF16, "vp")
            et = sb([128, 2 * 1024], BF16, "et")
            pvf = sb([65, 512], F32, "pvf")
            at = sb([128, 128], BF16, "at")
            rc = sb([128, 1], F32, "rc")
            apm = sb([128, T], BF16, "apm")
            self.op("gpsimd", lambda e: e.memset(vp, 1.0))
            self.op("gpsimd", lambda e: e.memset(at, 0.0))
            self.op("gpsimd", lambda e: e.memset(apm, 0.0))
            self.op("gpsimd", lambda e: e.memset(kp[64:65, :], 1.0))
            self.flush()

            pair_mm = {}
            snap = self.dma_snapshot()
            apm_spill = None
            exp_marks = {}
            vpA = vp.rearrange("r (b kt c) -> r b kt c", b=2, c=130)
            for p in range(NP):
                bp = p % 2
                wts = [(self.sMM, pair_mm.get(p - 2))] + (snap if p == 0 else [])
                for hh in range(2):
                    co = (2 * bp + hh) * T
                    self.sload("ldp", p, "sync", qp[0:64, co:co + T],
                               self.qs[p][hh * 64:(hh + 1) * 64, :], wts)
                    wts = []
                    self.sload("ldp", p, "sync", qp[64:65, co:co + T],
                               self.mrow[2 * p + hh:2 * p + hh + 1, :])
                    self.sload("ldp", p, "sync", kp[0:64, co:co + T],
                               self.ks[p][hh * 64:(hh + 1) * 64, :])
                self.sload("ldp", p, "sync", vpA[:, bp, :, 0:64],
                           self.vs[p].rearrange("(kt r) c -> r kt c", r=128)[:, :, 0:64])
                lall = self.sload("ldp", p, "sync", vpA[:, bp, :, 65:129],
                                  self.vs[p].rearrange("(kt r) c -> r kt c", r=128)[:, :, 64:128])
                for hh in range(2):
                    co = (2 * bp + hh) * T
                    hs = slice(hh * 64, (hh + 1) * 64)
                    for qc in range(NC):
                        pv_start = True
                        pend = None
                        for g in range(KT // 2 + 1):
                            if g < KT // 2:
                                psg = ps_s0 if g % 2 == 0 else ps_s1
                                wts = [lall]
                                prev = exp_marks.get(("slot", g % 2))
                                if prev is not None:
                                    wts.append((self.sAC, prev))
                                for kk in range(2):
                                    kt = 2 * g + kk
                                    self.op("tensor", lambda e, psg=psg, kk=kk, kt=kt,
                                            co=co, qc=qc: e.matmul(
                                                psg[:, kk * 512:(kk + 1) * 512],
                                                kp[0:65, co + kt * 128: co + (kt + 1) * 128],
                                                qp[0:65, co + qc * 512: co + (qc + 1) * 512],
                                                start=True, stop=True, skip_group_check=True),
                                            wts if kk == 0 else (), self.sMM)
                                    wts = ()
                                qk_mm = self.sMM.v
                                wts = [(self.sMM, qk_mm)]
                                pvprev = exp_marks.get(("et", g % 2))
                                if pvprev is not None:
                                    wts.append((self.sMM, pvprev))
                                ac = self.op("scalar", lambda e, psg=psg, g=g: e.activation(
                                    et[:, (g % 2) * 1024:(g % 2 + 1) * 1024], psg, AF.Exp),
                                    wts, self.sAC)
                                exp_marks[("slot", g % 2)] = ac
                            if pend is not None:
                                gp_, ac_ = pend
                                for kk in range(2):
                                    kt = 2 * gp_ + kk
                                    wts = [(self.sAC, ac_)]
                                    if pv_start and kk == 0:
                                        pe = exp_marks.get("pv_evac")
                                        if pe is not None:
                                            wts.append((self.sVE, pe))
                                    mm = self.op("tensor", lambda e, kt=kt, g=gp_, kk=kk, bp=bp,
                                                 hh=hh, st_=(pv_start and kk == 0),
                                                 sp=(kt == KT - 1): e.matmul(
                                                     ps_pv[0:65, :],
                                                     vpA[:, bp, kt, hh * 65:(hh + 1) * 65],
                                                     et[:, (g % 2) * 1024 + kk * 512:
                                                        (g % 2) * 1024 + (kk + 1) * 512],
                                                     start=st_, stop=sp, skip_group_check=True),
                                                 wts, self.sMM)
                                exp_marks[("et", gp_ % 2)] = mm
                                pv_start = False
                            if g < KT // 2:
                                pend = (g, ac)
                        ve_pv = self.op("vector", lambda e: e.tensor_copy(pvf, ps_pv[0:65, :]),
                                        [(self.sMM, mm)], self.sVE)
                        exp_marks["pv_evac"] = ve_pv
                        if self.debug and p == 0 and hh == 0 and qc == 0:
                            self.dma("gpsimd", self.dbg["d_pvf"], pvf, self.sGP,
                                     [(self.sVE, ve_pv)])
                        for i in range(4):
                            wts = [(self.sVE, ve_pv)]
                            prev_nf = exp_marks.get(("nf", i % 2))
                            if prev_nf is not None:
                                wts.append((self.sVE, prev_nf))
                            mmn = self.op("tensor", lambda e, i=i: e.transpose(
                                ps_nf[:, (i % 2) * 512:(i % 2) * 512 + 65],
                                pvf[0:65, i * 128:(i + 1) * 128],
                                self.ident_f[0:65, 0:65]), wts, self.sMM)
                            ve_rc = self.op("vector", lambda e, i=i: e.reciprocal(
                                rc, ps_nf[:, (i % 2) * 512 + 64:(i % 2) * 512 + 65]),
                                [(self.sMM, mmn)], self.sVE)
                            wts2 = []
                            prev_bt = exp_marks.get("bt")
                            if prev_bt is not None:
                                wts2.append((self.sMM, prev_bt))
                            wts2.append((self.sVE, ve_rc))
                            ve_at = self.op("vector", lambda e, i=i, hs=hs: e.tensor_scalar(
                                at[:, hs], ps_nf[:, (i % 2) * 512:(i % 2) * 512 + 64],
                                rc, None, ALU.mult), wts2, self.sVE)
                            exp_marks[("nf", i % 2)] = ve_at
                            wbt = [(self.sVE, ve_at)]
                            prev_ap = exp_marks.get("apm")
                            if prev_ap is not None:
                                wbt.append((self.sVE, prev_ap))
                            mmb = self.op("tensor", lambda e: e.transpose(
                                ps_bt, at, self.ident_b), wbt, self.sMM)
                            exp_marks["bt"] = mmb
                            wts3 = [(self.sMM, mmb)]
                            if apm_spill is not None and hh == 0 and qc == 0 and i == 0:
                                wts3.append(apm_spill)
                            ve_ap = self.op("vector", lambda e, i=i, hs=hs, qc=qc: e.tensor_copy(
                                apm[hs, qc * 512 + i * 128: qc * 512 + (i + 1) * 128],
                                ps_bt[hs, :]), wts3, self.sVE)
                            exp_marks["apm"] = ve_ap
                pair_mm[p] = self.sMM.v
                apm_spill = self.sload("afms", p, "sync", self.afm[p], apm,
                                       [(self.sVE, exp_marks["apm"])])
            self.flush()
            if self.debug:
                self.op("gpsimd", lambda e, v=self.sST.v: e.wait_ge(self.sST.h, v))
                self.dma("gpsimd", self.dbg["d_q"], self.qs, self.sGP)
                self.dma("gpsimd", self.dbg["d_k"], self.ks, self.sGP)
                self.dma("gpsimd", self.dbg["d_vs"], self.vs, self.sGP)
                self.dma("gpsimd", self.dbg["d_afm"], self.afm, self.sGP)
                self.flush()
        self.p1_stack.close()

    def p1d(self):
        nc, T, NC, NT = self.nc, self.T, self.NC, self.NT
        st = contextlib.ExitStack()
        sb = lambda shape, dt, name: self.sbl(st, shape, dt, name)
        with st:
            ps_o = self.psum(st, [128, 2 * 512], F32, "ps_o")
            ps_trf = self.psum(st, [128, 1024], F32, "ps_trf")
            ps_trb = self.psum(st, [128, 2048], BF16, "ps_trb")
            wo_sb = sb([128, NP * D], BF16, "wo_sb")
            af = sb([128, 2 * NP * 512], BF16, "af")
            of = sb([128, NP * 512], F32, "of")
            xa = sb([128, 2 * D], F32, "xa")
            xn = sb([128, 2 * D], F32, "xn")
            lnt = sb([128, D], BF16, "lnt")
            spl = sb([128, NP * 128], BF16, "spl")
            ldwo = self.dma("gpsimd", wo_sb.rearrange("p (dc o) -> p dc o", dc=NP),
                            self.Wo.rearrange("(dc p) o -> p dc o", p=128), self.sGP)
            trf = MK.TrPool(self, ps_trf, 512, self.ident_f)
            trb = MK.TrPool(self, ps_trb, 1024, self.ident_b)
            snap = self.dma_snapshot()
            af_mm = {}
            of_free = {}
            spl_dma = None
            xn_free = {}
            for tc in range(NC):
                b = tc % 2
                wts = [(self.sMM, af_mm.get(tc - 2))] + (snap if tc == 0 else [])
                lda = self.sload("af2", tc, "sync",
                                 af.rearrange("d (b p t) -> d b p t", b=2, p=NP)[:, b],
                                 self.afm.rearrange("p d t -> d p t")[:, :, tc * 512:(tc + 1) * 512],
                                 wts)
                for oc in range(NP):
                    wts = [lda, (self.sGP, ldwo)]
                    if of_free.get(oc % 2) is not None:
                        wts.append((self.sAC, of_free[oc % 2]))
                    for dc in range(NP):
                        mm = self.op("tensor", lambda e, oc=oc, dc=dc, b=b: e.matmul(
                            ps_o[:, (oc % 2) * 512:(oc % 2 + 1) * 512],
                            wo_sb[:, dc * D + oc * 128: dc * D + oc * 128 + 128],
                            af[:, (b * NP + dc) * 512:(b * NP + dc + 1) * 512],
                            start=(dc == 0), stop=(dc == NP - 1), skip_group_check=True),
                            wts if dc == 0 else (), self.sMM)
                    ac_ev = self.op("scalar", lambda e, oc=oc: e.activation(
                        of[:, oc * 512:(oc + 1) * 512], ps_o[:, (oc % 2) * 512:(oc % 2 + 1) * 512],
                        AF.Identity), [(self.sMM, mm)], self.sAC)
                    of_free[oc % 2] = ac_ev
                af_mm[tc] = self.sMM.v
                ac_all = self.sAC.v
                for ts_ in range(4):
                    tt = tc * 4 + ts_
                    bx = tt % 2
                    wx = [(self.sVE, xn_free.get(tt - 2))]
                    ldx = self.sload("xa1", tt, "sync", xa[:, bx * D:(bx + 1) * D],
                                     self.xin[tt * 128:(tt + 1) * 128, :], wx)
                    for oc in range(NP):
                        dst_ps, mm, i = trf.tr(of[:, oc * 512 + ts_ * 128: oc * 512 + (ts_ + 1) * 128],
                                               [(self.sAC, ac_all)])
                        wxn = [(self.sMM, mm), ldx]
                        if oc == 0 and self._xst.get(tt - 2) is not None:
                            wxn.append(self._xst[tt - 2])
                        ev = self.op("vector", lambda e, d=dst_ps, oc=oc, bx=bx: e.tensor_tensor(
                            xn[:, bx * D + oc * 128: bx * D + (oc + 1) * 128],
                            xa[:, bx * D + oc * 128: bx * D + (oc + 1) * 128],
                            d[:, 0:128], ALU.add),
                            wxn, self.sVE)
                        trf.done(i, (self.sVE, ev))
                    self._xst[tt] = self.sload("xst", tt, "sync",
                                               self.xres[tt * 128:(tt + 1) * 128, :],
                                               xn[:, bx * D:(bx + 1) * D], [(self.sVE, ev)])
                    ve_ln = self.ln_tile(xn[:, bx * D:(bx + 1) * D], lnt,
                                         extra_src_waits=[(self.sVE, ev)])
                    for dc in range(NP):
                        wts = []
                        if dc == 0 and spl_dma is not None:
                            wts.append(spl_dma)
                        dst_ps, mm, i = trb.tr(lnt[:, dc * 128:(dc + 1) * 128],
                                               [(self.sVE, ve_ln)] + wts)
                        ev2 = self.op("vector", lambda e, d=dst_ps, dc=dc: e.tensor_copy(
                            spl[:, dc * 128:(dc + 1) * 128], d[:, 0:128]),
                            [(self.sMM, mm)], self.sVE)
                        trb.done(i, (self.sVE, ev2))
                    xn_free[tt] = ve_ln
                    spl_dma = self.sload(
                        "spl", tt, "sync",
                        self.lnfm.rearrange("p d t -> d p t")[:, :, tt * 128:(tt + 1) * 128],
                        spl.rearrange("d (p t) -> d p t", p=NP), [(self.sVE, ev2)])
            self.flush()
            if self.debug:
                self.op("gpsimd", lambda e, v=self.sST.v: e.wait_ge(self.sST.h, v))
                self.dma("gpsimd", self.dbg["d_x1"], self.xres, self.sGP)
                self.dma("gpsimd", self.dbg["d_lnfm"], self.lnfm, self.sGP)
                self.flush()

    def all_sems(self):
        out = [getattr(self, "s" + nm) for nm in
               ("LDX", "LDW", "LDA", "ST", "MM", "VE", "AC", "GP", "GPE")]
        for sems in getattr(self, "_streams", {}).values():
            out.extend(sems)
        return out

    def reset_sems(self):
        self.dma_drain()
        self.flush()
        sems = [self.sMM, self.sVE, self.sAC, self.sGPE]
        for s in sems:
            self.op("sync", lambda e, h=s.h: e.sem_clear(h))
        self.flush()
        for s in sems:
            s.base = s.v

    def _phase_gap(self):
        self.reset_sems()

    def build(self):
        self.declare_io()
        with self.ctx:
            self.alloc_global()
            self.p0()
            if 1 in self.phases:
                self.p1_stack = contextlib.ExitStack()
                self.nrm = self.sbl(self.p1_stack, [16, 2 * self.T], F32, "nrm")
                self.mrow = self.sbl(self.p1_stack, [16, self.T], BF16, "mrow")
                self.p1b()
                self._phase_gap()
                self.p1c()
                self._phase_gap()
                self.p1d()
                self._phase_gap()
            if 2 in self.phases:
                self.ffn(first=True)
                self._phase_gap()
            if 3 in self.phases:
                self.p3()
                self._phase_gap()
            if 4 in self.phases:
                self.ffn(first=False)
            self.finalize()
        return self.nc

    def ffn(self, first):
        nc, T = self.nc, self.T
        SLAB, NSLAB, ST, SC = self.SLAB, self.NSLAB, self.ST, self.SC
        st = contextlib.ExitStack()
        sb = lambda shape, dt, name: self.sbl(st, shape, dt, name)
        with st:
            ps_gu = self.psum(st, [128, 2048], F32, "ps_gu")
            ps_d = self.psum(st, [128, 1024], F32, "ps_d")
            ps_m = self.psum(st, [128, 512], F32, "ps_m")
            ps_tb = self.psum(st, [128, 1024], BF16, "ps_tb")
            lnslab = sb([128, 2 * NP * SLAB], BF16, "lnslab")
            rw_b = sb([128, NP * 64], BF16, "rw_b")
            rt = sb([128, 6 * 64], F32, "rt")
            rc = sb([128, 8], F32, "rc")
            w_fm = sb([64, SLAB], BF16, "w_fm")
            wrow = sb([1, 2 * SLAB], BF16, "wrow")
            wg_sb = sb([128, 2 * 8 * 512], BF16, "wg_sb")
            wu_sb = sb([128, 2 * 8 * 512], BF16, "wu_sb")
            wd_sb = sb([128, 2 * 4 * 1024], BF16, "wd_sb")
            sgate = sb([128, 2 * 512], BF16, "sgate")
            msk = sb([128, SC * 512], BF16, "msk")
            act_fm = sb([128, 2 * 4 * SLAB], BF16, "act_fm")
            acc = sb([128, ST * 1024], F32, "acc")
            xa = sb([128, 2 * D], F32, "xa")
            xn = sb([128, 2 * D], F32, "xn")
            lnt = sb([128, D], BF16, "lnt")
            prm = sb([128, D], BF16, "prm")
            spl = sb([128, NP * 128], BF16, "spl")
            AFsig = AF.Sigmoid
            AFsilu = AF.Silu
            x_src = self.xres
            x_dst = self.xres if first else self.yout
            ln_dst = self.lnpm if first else None

            snap = self.dma_snapshot()
            ldrw = self.dma("gpsimd", rw_b.rearrange("p (dc e) -> p dc e", dc=NP),
                            self.rw.rearrange("(dc p) e -> p dc e", p=128), self.sGP,
                            snap)
            slab_mm_done = {}
            wrow_dma = None
            xn_marks = {}
            for s in range(NSLAB):
                bs = s % 2
                lnsl = lnslab[:, bs * NP * SLAB:(bs + 1) * NP * SLAB]
                wts = snap + [(self.sMM, slab_mm_done.get(s - 2))]
                lds = self.sload("lns", s, "sync",
                                 lnsl.rearrange("p (dc t) -> p dc t", dc=NP),
                                 self.lnfm.rearrange("p d t -> d p t")[:, :, s * SLAB:(s + 1) * SLAB],
                                 wts, n=2)
                self.act_load([AFsig, AF.Identity])
                aff0, affw, eqm, k01, wunn, wfin = (rt[:, i * 64:(i + 1) * 64] for i in range(6))
                wfm_mark = None
                for t_ in range(ST):
                    wts = [lds, (self.sGP, ldrw)]
                    if wfm_mark is not None:
                        wts.append((self.sVE, wfm_mark))
                    for dc in range(NP):
                        mmr = self.op("tensor", lambda e, dc=dc, t_=t_, lnsl=lnsl: e.matmul(
                            ps_m[:, 0:64],
                            lnsl[:, dc * SLAB + t_ * 128: dc * SLAB + (t_ + 1) * 128],
                            rw_b[:, dc * 64:(dc + 1) * 64],
                            start=(dc == 0), stop=(dc == NP - 1), skip_group_check=True),
                            wts if dc == 0 else (), self.sMM)
                    aca = self.op("scalar", lambda e: e.activation(aff0, ps_m[:, 0:64], AFsig),
                                  [(self.sMM, mmr)], self.sAC)
                    vew = self.op("vector", lambda e: e.tensor_copy(affw, aff0),
                                  [(self.sAC, aca)], self.sVE)
                    for it in range(8):
                        vem = self.op("vector", lambda e: e.reduce_max(
                            rc[:, 0:1], affw, axis=AX), inc=self.sVE)
                        vee = self.op("vector", lambda e: e.tensor_scalar(
                            eqm, affw, rc[:, 0:1], self.cst[:, 12:13],
                            ALU.is_equal, ALU.mult), [(self.sVE, vem)], self.sVE)
                        vew = self.op("vector", lambda e: e.tensor_tensor(
                            affw, affw, eqm, ALU.add), [(self.sVE, vee)], self.sVE)
                    vek = self.op("vector", lambda e: e.tensor_scalar(
                        k01, aff0, rc[:, 0:1], None, ALU.is_ge), [(self.sVE, vew)], self.sVE)
                    veu = self.op("vector", lambda e: e.tensor_tensor(
                        wunn, aff0, k01, ALU.mult), [(self.sVE, vek)], self.sVE)
                    ves = self.op("vector", lambda e: e.reduce_sum(
                        rc[:, 1:2], wunn, axis=AX), [(self.sVE, veu)], self.sVE)
                    ves2 = self.op("vector", lambda e: e.tensor_scalar(
                        rc[:, 2:3], rc[:, 1:2], self.cst[:, 5:6], None, ALU.add),
                        [(self.sVE, ves)], self.sVE)
                    ver = self.op("vector", lambda e: e.reciprocal(rc[:, 3:4], rc[:, 2:3]),
                                  [(self.sVE, ves2)], self.sVE)
                    vef = self.op("vector", lambda e: e.tensor_scalar(
                        wfin, wunn, rc[:, 3:4], None, ALU.mult), [(self.sVE, ver)], self.sVE)
                    wts = [(self.sVE, vef), (self.sAC, aca)]
                    if wrow_dma is not None:
                        wts.append(wrow_dma)
                    mmt = self.op("tensor", lambda e: e.transpose(
                        ps_m[0:64, 64:192], wfin, self.ident_f), wts, self.sMM)
                    wfm_mark = self.op("vector", lambda e, t_=t_: e.tensor_copy(
                        w_fm[:, t_ * 128:(t_ + 1) * 128], ps_m[0:64, 64:192]),
                        [(self.sMM, mmt)], self.sVE)
                if self.debug:
                    self.dma("gpsimd", self.dbg["d_w"][:, s * SLAB:(s + 1) * SLAB],
                             w_fm, self.sGP, [(self.sVE, wfm_mark)])
                self.act_load([AFsilu, AF.Identity])
                mmE = {}
                slot_marks = {}
                for e_i in range(-1, self.n_exp):
                    be = (e_i + 1) % 2
                    wts = [(self.sMM, mmE.get(e_i - 2))]
                    if e_i < 0:
                        srcs = (self.sg, self.su, self.sd)
                    else:
                        srcs = (self.eg[e_i], self.eu[e_i], self.ed[e_i])
                    ldg = self.sload("wg", be, "gpsimd",
                                      wg_sb.rearrange("p (b dc i) -> p b dc i", b=2, dc=NP)[:, be],
                                      srcs[0].rearrange("(dc p) i -> p dc i", p=128), wts)
                    ldu = self.sload("wu", be, "gpsimd",
                                     wu_sb.rearrange("p (b dc i) -> p b dc i", b=2, dc=NP)[:, be],
                                     srcs[1].rearrange("(dc p) i -> p dc i", p=128))
                    ldd = self.sload("wd", be, "gpsimd",
                                     wd_sb.rearrange("p (b ic h) -> p b ic h", b=2, ic=4)[:, be],
                                     srcs[2].rearrange("(ic p) h -> p ic h", p=128))
                    if e_i >= 0:
                        wrow_dma = self.sload("wr", be, "sync",
                                              wrow[0:1, be * SLAB:(be + 1) * SLAB],
                                              w_fm[e_i:e_i + 1, :],
                                              [(self.sVE, wfm_mark), (self.sMM, mmE.get(e_i - 2))])
                    for tc in range(SC):
                        if e_i >= 0:
                            wts = [wrow_dma]
                            pm = slot_marks.get(("msk", tc))
                            if pm is not None:
                                wts.append((self.sVE, pm))
                            mmm = self.op("tensor", lambda e, tc=tc, be=be: e.matmul(
                                ps_m[:, 0:512], self.ones_row,
                                wrow[0:1, be * SLAB + tc * 512: be * SLAB + (tc + 1) * 512],
                                start=True, stop=True, skip_group_check=True),
                                wts, self.sMM)
                            vemk = self.op("vector", lambda e, tc=tc: e.tensor_copy(
                                msk[:, tc * 512:(tc + 1) * 512], ps_m[:, 0:512]),
                                [(self.sMM, mmm)], self.sVE)
                            slot_marks[("msk", tc)] = vemk
                        for ic in range(4):
                            gslot = ic % 2
                            uslot = 2 + ic % 2
                            wts = [ldg]
                            pm = slot_marks.get(("g", gslot))
                            if pm is not None:
                                wts.append((self.sAC, pm))
                            for dc in range(NP):
                                mmg = self.op("tensor", lambda e, dc=dc, ic=ic, tc=tc, be=be,
                                              gs=gslot, lnsl=lnsl: e.matmul(
                                    ps_gu[:, gs * 512:(gs + 1) * 512],
                                    wg_sb[:, (be * NP + dc) * 512 + ic * 128:
                                          (be * NP + dc) * 512 + (ic + 1) * 128],
                                    lnsl[:, dc * SLAB + tc * 512: dc * SLAB + (tc + 1) * 512],
                                    start=(dc == 0), stop=(dc == NP - 1),
                                    skip_group_check=True),
                                    wts if dc == 0 else (), self.sMM)
                            wts = [ldu]
                            pm = slot_marks.get(("u", uslot))
                            if pm is not None:
                                wts.append((self.sVE, pm))
                            for dc in range(NP):
                                mmu = self.op("tensor", lambda e, dc=dc, ic=ic, tc=tc, be=be,
                                              us=uslot, lnsl=lnsl: e.matmul(
                                    ps_gu[:, us * 512:(us + 1) * 512],
                                    wu_sb[:, (be * NP + dc) * 512 + ic * 128:
                                          (be * NP + dc) * 512 + (ic + 1) * 128],
                                    lnsl[:, dc * SLAB + tc * 512: dc * SLAB + (tc + 1) * 512],
                                    start=(dc == 0), stop=(dc == NP - 1),
                                    skip_group_check=True),
                                    wts if dc == 0 else (), self.sMM)
                            wsg = [(self.sMM, mmg)]
                            pm = slot_marks.get(("u", 2 + gslot))
                            if pm is not None:
                                wsg.append((self.sVE, pm))
                            acg = self.op("scalar", lambda e, gs=gslot, ic=ic: e.activation(
                                sgate[:, gs * 512:(gs + 1) * 512],
                                ps_gu[:, gs * 512:(gs + 1) * 512], AFsilu),
                                wsg, self.sAC)
                            slot_marks[("g", gslot)] = acg
                            veau = self.op("vector", lambda e, us=uslot, gs=gslot: e.tensor_tensor(
                                self.sq[:, 0:512], ps_gu[:, us * 512:(us + 1) * 512],
                                sgate[:, gs * 512:(gs + 1) * 512], ALU.mult),
                                [(self.sMM, mmu), (self.sAC, acg)], self.sVE)
                            slot_marks[("u", uslot)] = veau
                            wts = [(self.sVE, veau)]
                            if e_i >= 0:
                                wts.append((self.sVE, slot_marks[("msk", tc)]))
                            pm = slot_marks.get(("act", be))
                            if pm is not None and tc == 0 and ic == 0:
                                wts.append(pm)
                            if e_i >= 0:
                                veact = self.op("vector", lambda e, be=be, ic=ic, tc=tc: e.tensor_tensor(
                                    act_fm[:, (be * 4 + ic) * SLAB + tc * 512:
                                           (be * 4 + ic) * SLAB + (tc + 1) * 512],
                                    self.sq[:, 0:512], msk[:, tc * 512:(tc + 1) * 512],
                                    ALU.mult), wts, self.sVE)
                            else:
                                veact = self.op("vector", lambda e, be=be, ic=ic, tc=tc: e.tensor_copy(
                                    act_fm[:, (be * 4 + ic) * SLAB + tc * 512:
                                           (be * 4 + ic) * SLAB + (tc + 1) * 512],
                                    self.sq[:, 0:512]), wts, self.sVE)
                        slot_marks[("acttc", tc)] = veact
                    for t_ in range(ST):
                        tc = t_ // (ST // SC) if SC > 1 else 0
                        for hc in range(2):
                            dslot = (t_ * 2 + hc) % 2
                            wts = [(self.sVE, slot_marks[("acttc", tc)]), ldd]
                            pm = slot_marks.get(("d", dslot))
                            if pm is not None:
                                wts.append((self.sVE, pm))
                            for ic in range(4):
                                mmd = self.op("tensor", lambda e, ic=ic, t_=t_, hc=hc, be=be,
                                              ds=dslot: e.matmul(
                                    ps_d[:, ds * 512:(ds + 1) * 512],
                                    act_fm[:, (be * 4 + ic) * SLAB + t_ * 128:
                                           (be * 4 + ic) * SLAB + (t_ + 1) * 128],
                                    wd_sb[:, (be * 4 + ic) * 1024 + hc * 512:
                                          (be * 4 + ic) * 1024 + (hc + 1) * 512],
                                    start=(ic == 0), stop=(ic == 3),
                                    skip_group_check=True),
                                    wts if ic == 0 else (), self.sMM)
                            if e_i < 0:
                                ved = self.op("vector", lambda e, t_=t_, hc=hc, ds=dslot: e.tensor_copy(
                                    acc[:, t_ * 1024 + hc * 512: t_ * 1024 + (hc + 1) * 512],
                                    ps_d[:, ds * 512:(ds + 1) * 512]),
                                    [(self.sMM, mmd)], self.sVE)
                            else:
                                ved = self.op("vector", lambda e, t_=t_, hc=hc, ds=dslot: e.tensor_tensor(
                                    acc[:, t_ * 1024 + hc * 512: t_ * 1024 + (hc + 1) * 512],
                                    acc[:, t_ * 1024 + hc * 512: t_ * 1024 + (hc + 1) * 512],
                                    ps_d[:, ds * 512:(ds + 1) * 512], ALU.add),
                                    [(self.sMM, mmd)], self.sVE)
                            slot_marks[("d", dslot)] = ved
                    slot_marks[("act", be)] = (self.sMM, self.sMM.v)
                    mmE[e_i] = self.sMM.v
                slab_mm_done[s] = self.sMM.v
                acc_done = ved
                self.act_load([AF.Exp, AF.Ln, AF.Square, AF.Identity])
                trb = MK.TrPool(self, ps_tb, 1024, self.ident_b, nslots=1)
                spl_dma = None
                for t_ in range(ST):
                    tt = s * ST + t_
                    bx = t_ % 2
                    wts = [(self.sVE, xn_marks.get(tt - 2))]
                    ldx = self.sload("xa1", tt, "sync", xa[:, bx * D:(bx + 1) * D],
                                     x_src[tt * 128:(tt + 1) * 128, :], wts)
                    wxn = [ldx, (self.sVE, acc_done)]
                    if self._xst.get(tt - 2) is not None:
                        wxn.append(self._xst[tt - 2])
                    vex = self.op("vector", lambda e, bx=bx, t_=t_: e.tensor_tensor(
                        xn[:, bx * D:(bx + 1) * D], xa[:, bx * D:(bx + 1) * D],
                        acc[:, t_ * 1024:(t_ + 1) * 1024], ALU.add),
                        wxn, self.sVE)
                    self._xst[tt] = self.sload("xst", tt, "sync",
                                               x_dst[tt * 128:(tt + 1) * 128, :],
                                               xn[:, bx * D:(bx + 1) * D], [(self.sVE, vex)])
                    if ln_dst is None:
                        xn_marks[tt] = vex
                        continue
                    ve_ln = self.ln_tile(xn[:, bx * D:(bx + 1) * D], lnt,
                                         extra_src_waits=[(self.sVE, vex)])
                    xn_marks[tt] = ve_ln
                    ve_pm = self.op("vector", lambda e: e.tensor_copy(
                        prm.rearrange("p (h di) -> p h di", h=NH),
                        lnt.rearrange("p (di h) -> p h di", h=NH)),
                        [(self.sVE, ve_ln)], self.sVE)
                    for dc in range(NP):
                        wts2 = [(self.sVE, ve_pm)]
                        if dc == 0 and spl_dma is not None:
                            wts2.append(spl_dma)
                        dst_ps, mm, i = trb.tr(prm[:, dc * 128:(dc + 1) * 128], wts2)
                        ev2 = self.op("vector", lambda e, d=dst_ps, dc=dc: e.tensor_copy(
                            spl[:, dc * 128:(dc + 1) * 128], d), [(self.sMM, mm)], self.sVE)
                        trb.done(i, (self.sVE, ev2))
                    spl_dma = self.sload(
                        "spl", tt, "sync",
                        ln_dst.rearrange("p d t -> d p t")[:, :, tt * 128:(tt + 1) * 128],
                        spl.rearrange("d (p t) -> d p t", p=NP), [(self.sVE, ev2)])
                if s % 2 == 1:
                    self.flush()
            if self.debug:
                self.op("gpsimd", lambda e, v=self.sST.v: e.wait_ge(self.sST.h, v))
                if first:
                    self.dma("gpsimd", self.dbg["d_x1"], self.xres, self.sGP)
                    self.dma("gpsimd", self.dbg["d_lnpm"], self.lnpm, self.sGP)
                self.flush()

    def p3(self):
        nc, T, NC, NT = self.nc, self.T, self.NC, self.NT
        KT = T // 128
        st = contextlib.ExitStack()
        sb = lambda shape, dt, name: self.sbl(st, shape, dt, name)
        with st:
            q_fm = sb([128, NP * T], BF16, "q_fm")
            k_fm = sb([128, NP * T], BF16, "k_fm")
            mrb = sb([1, T], BF16, "mrb")
            self.act_load([AF.Exp, AF.Ln, AF.Square, AF.Identity])

            _sv = self.serial
            self.serial = _sv or ("mix" in self.serial_parts) or ("mixmm" in self.serial_parts)
            with contextlib.ExitStack() as stm:
                sbm = lambda shape, dt, name: self.sbl(stm, shape, dt, name)
                ps_a = self.psum(stm, [128, 2048], F32, "ps_a")
                ps_tb = self.psum(stm, [128, 2048], BF16, "ps_tb")
                lnp = sbm([128, 2 * T], BF16, "lnp")
                w3 = sbm([128, 3 * NH * 128], BF16, "w3")
                cs = sbm([128, 2 * 2 * 512], F32, "cs")
                vst = sbm([128, 512], BF16, "vst")
                vfm = sbm([128, 512], BF16, "vfm")
                snap = self.dma_snapshot()
                for ki, wsrc in enumerate((self.wq3, self.wk3, self.wv3)):
                    for hb in range(2):
                        for dd in range(2):
                            self.dma("gpsimd",
                                     w3.rearrange("p (k h d) -> p k h d", k=3, h=NH)[
                                         hb * 64:(hb + 1) * 64, ki, :, dd * 64:(dd + 1) * 64],
                                     wsrc.rearrange("h di do -> di h do"),
                                     self.sGP, snap)
                ldw3 = self.sGP.v
                veq = self.op("vector", lambda e: e.tensor_scalar(
                    w3[:, 0:NH * 128], w3[:, 0:NH * 128], self.cst[:, 9:10], None,
                    ALU.mult), [(self.sGP, ldw3)], self.sVE)
                lnp_mark = {}
                vdma = None
                q2_mark = None
                for p in range(NP):
                    bp = p % 2
                    wts = snap + [(self.sVE, veq)]
                    if p >= 2:
                        wts.append((self.sMM, lnp_mark[p - 2]))
                    ldp = self.sload("lnp", p, "sync", lnp[:, bp * T:(bp + 1) * T],
                                     self.lnpm[p], wts)
                    for tc in range(NC):
                        bc = tc % 2
                        lcs = None
                        csw = []
                        pm = lnp_mark.get(("cs", bc))
                        if pm is not None:
                            csw.append((self.sVE, pm))
                        for hb in range(2):
                            self.sload("cs", tc, "sync",
                                       cs[hb * 64:(hb + 1) * 64, bc * 1024:bc * 1024 + 512],
                                       self.cosP[p, :, tc * 512:(tc + 1) * 512], csw)
                            csw = []
                            lcs = self.sload("cs", tc, "sync",
                                             cs[hb * 64:(hb + 1) * 64,
                                                bc * 1024 + 512:bc * 1024 + 1024],
                                             self.sinP[p, :, tc * 512:(tc + 1) * 512])
                        for ki, dst in ((0, q_fm), (1, k_fm)):
                            sA = (tc % 2) * 1024
                            sB = sA + 512
                            wts = [ldp]
                            pm = lnp_mark.get(("rope", tc % 2))
                            if pm is not None:
                                wts.append((self.sVE, pm))
                            mmA = self.op("tensor", lambda e, ki=ki, p=p, tc=tc, sA=sA, bp=bp: e.matmul(
                                ps_a[:, sA:sA + 512],
                                w3[0:64, (ki * NH + 2 * p) * 128:(ki * NH + 2 * p) * 128 + 128],
                                lnp[0:64, bp * T + tc * 512: bp * T + (tc + 1) * 512],
                                start=True, stop=True, skip_group_check=True),
                                wts, self.sMM)
                            mmB = self.op("tensor", lambda e, ki=ki, p=p, tc=tc, sB=sB, bp=bp: e.matmul(
                                ps_a[:, sB:sB + 512],
                                w3[64:128, (ki * NH + 2 * p + 1) * 128:(ki * NH + 2 * p + 1) * 128 + 128],
                                lnp[64:128, bp * T + tc * 512: bp * T + (tc + 1) * 512],
                                start=True, stop=True, skip_group_check=True),
                                (), self.sMM)
                            _s2 = self.serial
                            self.serial = _s2 or ("rope" in self.serial_parts)
                            co0 = cs[0:64, bc * 1024:bc * 1024 + 512]
                            co1 = cs[64:128, bc * 1024:bc * 1024 + 512]
                            si0 = cs[0:64, bc * 1024 + 512:bc * 1024 + 1024]
                            si1 = cs[64:128, bc * 1024 + 512:bc * 1024 + 1024]
                            A0, A1 = ps_a[0:64, sA:sA + 512], ps_a[64:128, sA:sA + 512]
                            B0, B1 = ps_a[0:64, sB:sB + 512], ps_a[64:128, sB:sB + 512]
                            dlo = dst[0:64, p * T + tc * 512: p * T + (tc + 1) * 512]
                            dhi = dst[64:128, p * T + tc * 512: p * T + (tc + 1) * 512]
                            wts2 = [(self.sMM, mmB), lcs]
                            self.op("vector", lambda e, A0=A0, co0=co0: e.tensor_tensor(
                                self.scr[0:64, 0:512], A0, co0, ALU.mult), wts2)
                            self.op("vector", lambda e, B0=B0, si0=si0: e.tensor_tensor(
                                self.scr2[0:64, 0:512], B0, si0, ALU.mult))
                            self.op("vector", lambda e, dlo=dlo: e.tensor_tensor(
                                dlo, self.scr[0:64, 0:512], self.scr2[0:64, 0:512],
                                ALU.subtract))
                            self.op("vector", lambda e, A1=A1, si1=si1: e.tensor_tensor(
                                self.scr[64:128, 0:512], A1, si1, ALU.mult))
                            self.op("vector", lambda e, B1=B1, co1=co1: e.tensor_tensor(
                                self.scr2[64:128, 0:512], B1, co1, ALU.mult))
                            ve_r = self.op("vector", lambda e, dhi=dhi: e.tensor_tensor(
                                dhi, self.scr[64:128, 0:512], self.scr2[64:128, 0:512],
                                ALU.add), inc=self.sVE)
                            lnp_mark[("rope", tc % 2)] = ve_r
                            lnp_mark[("cs", tc % 2)] = ve_r
                            self.serial = _s2
                        sV = (tc % 2) * 1024
                        _s3 = self.serial
                        self.serial = _s3 or ("vmix" in self.serial_parts)
                        wts = [ldp, (self.sVE, lnp_mark.get(("rope", tc % 2)))]
                        mva = self.op("tensor", lambda e, p=p, tc=tc, sV=sV, bp=bp: e.matmul(
                            ps_a[:, sV:sV + 512],
                            w3[0:64, (2 * NH + 2 * p) * 128:(2 * NH + 2 * p) * 128 + 128],
                            lnp[0:64, bp * T + tc * 512: bp * T + (tc + 1) * 512],
                            start=True, stop=True, skip_group_check=True),
                            [w for w in wts if w[1] is not None], self.sMM)
                        mvb = self.op("tensor", lambda e, p=p, tc=tc, sV=sV, bp=bp: e.matmul(
                            ps_a[:, sV + 512:sV + 1024],
                            w3[64:128, (2 * NH + 2 * p + 1) * 128:(2 * NH + 2 * p + 1) * 128 + 128],
                            lnp[64:128, bp * T + tc * 512: bp * T + (tc + 1) * 512],
                            start=True, stop=True, skip_group_check=True),
                            (), self.sMM)
                        wts = [(self.sMM, mvb)]
                        pm = lnp_mark.get(("vfm", tc % 2))
                        if pm is not None:
                            wts.append((self.sMM, pm))
                        ve1 = self.op("vector", lambda e, sV=sV: e.tensor_copy(
                            vfm[0:64, 0:512], ps_a[0:64, sV:sV + 512]), wts, self.sVE)
                        ve2 = self.op("vector", lambda e, sV=sV: e.tensor_copy(
                            vfm[64:128, 0:512], ps_a[64:128, sV + 512:sV + 1024]),
                            inc=self.sVE)
                        lnp_mark[("rope", tc % 2)] = ve2
                        for sub in range(4):
                            wts = [(self.sVE, ve2)]
                            if vdma is not None and sub == 0:
                                wts.append(vdma)
                            pm = lnp_mark.get(("vtb", sub % 2))
                            if pm is not None:
                                wts.append((self.sVE, pm))
                            mtv = self.op("tensor", lambda e, sub=sub: e.transpose(
                                ps_tb[:, (sub % 2) * 1024:(sub % 2) * 1024 + 128],
                                vfm[:, sub * 128:(sub + 1) * 128], self.ident_b),
                                wts, self.sMM)
                            vev = self.op("vector", lambda e, sub=sub: e.tensor_copy(
                                vst[:, sub * 128:(sub + 1) * 128],
                                ps_tb[:, (sub % 2) * 1024:(sub % 2) * 1024 + 128]),
                                [(self.sMM, mtv)], self.sVE)
                            lnp_mark[("vtb", sub % 2)] = vev
                        lnp_mark[("vfm", tc % 2)] = mtv
                        vdma = self.sload(
                            "vd", 0, "sync",
                            self.vb.rearrange("(a sub r) (pp c) -> r a sub pp c",
                                              sub=4, r=128, pp=NP)[:, tc, :, p, :],
                            vst.rearrange("r (sub c) -> r sub c", sub=4),
                            [(self.sVE, vev)], n=1)
                        self.serial = _s3
                    lnp_mark[p] = self.sMM.v
                self.flush()


            self.serial = _sv or ("attn" in self.serial_parts)
            self.dma_drain()
            with contextlib.ExitStack() as sta:
                sba = lambda shape, dt, name: self.sbl(sta, shape, dt, name)
                ps_s = self.psum(sta, [128, 1024], F32, "ps_s")
                ps_pv = self.psum(sta, [128, 1024], F32, "ps_pv")
                ps_tf = self.psum(sta, [128, 1024], F32, "ps_tf")
                ps_sm = ps_tf
                ps_tb = self.psum(sta, [128, 2048], BF16, "ps_tb")
                vb_snap = self.dma_snapshot()
                with contextlib.ExitStack() as stn:
                    q2b = self.sbl(stn, [128, 2048], BF16, "q2b")
                    mrq = self.sbl(stn, [1, T], F32, "mrq")
                    last_mmn = None
                    vee = None
                    vemx = None
                    for half, fm in enumerate((k_fm, q_fm)):
                        for rnd in range(2):
                            if rnd * 4 >= NC:
                                continue
                            for p in range(NP):
                                wq2 = []
                                if half == 0 and rnd == 0 and p == 0:
                                    wq2.append((self.sVE, self.sVE.v))
                                if half == 1 and rnd == 0 and p == 0:
                                    wq2.append((self.sVE, vemx))
                                if last_mmn is not None:
                                    wq2.append((self.sMM, last_mmn))
                                if vee is not None and p == 0:
                                    wq2.append((self.sVE, vee))
                                rw_ = min(2048, T)
                                acq = self.op("scalar", lambda e, fm=fm, p=p, rnd=rnd, rw_=rw_: e.activation(
                                    q2b[:, 0:rw_],
                                    fm[:, p * T + rnd * 2048: p * T + rnd * 2048 + rw_],
                                    AF.Square), wq2, self.sAC)
                                for tci in range(4):
                                    tc = rnd * 4 + tci
                                    if tc >= NC:
                                        continue
                                    psn = ps_pv if tci < 2 else ps_s
                                    mmn = self.op("tensor", lambda e, tci=tci, p=p, psn=psn: e.matmul(
                                        psn[:, (tci % 2) * 512:(tci % 2 + 1) * 512][0:1, :],
                                        self.ones_col, q2b[:, tci * 512:(tci + 1) * 512],
                                        start=(p == 0), stop=(p == NP - 1),
                                        skip_group_check=True),
                                        [(self.sAC, acq)], self.sMM)
                                last_mmn = mmn
                            for tci in range(4):
                                tc = rnd * 4 + tci
                                if tc >= NC:
                                    continue
                                psn = ps_pv if tci < 2 else ps_s
                                vee = self.op("vector", lambda e, tci=tci, tc=tc, psn=psn: e.tensor_copy(
                                    mrq[0:1, tc * 512:(tc + 1) * 512],
                                    psn[0:1, (tci % 2) * 512:(tci % 2 + 1) * 512]),
                                    [(self.sMM, mmn)], self.sVE)
                        if half == 0:
                            vemx = self.op("vector", lambda e: e.reduce_max(
                                self.col[0:1, 7:8], mrq, axis=AX),
                                [(self.sVE, vee)], self.sVE)
                    vem2 = self.op("vector", lambda e: e.tensor_scalar(
                        mrq, mrq, self.col[0:1, 7:8], None, ALU.mult),
                        [(self.sVE, vee)], self.sVE)
                    self.op("scalar", lambda e: e.activation(mrq, mrq, AF.Ln),
                            [(self.sVE, vem2)])
                    acm = self.op("scalar", lambda e: e.activation(
                        mrq, mrq, AF.Exp, scale=0.5), inc=self.sAC)
                    vemr = self.op("vector", lambda e: e.tensor_scalar(
                        mrb, mrq, self.cst[0:1, 4:5], self.cst[0:1, 8:9],
                        ALU.mult, ALU.add), [(self.sAC, acm)], self.sVE)
                    if self.debug:
                        self.dma("gpsimd", self.dbg["d_qb"].rearrange("c p t -> p c t"),
                                 q_fm.rearrange("p (c t) -> p c t", c=NP), self.sGP,
                                 [(self.sVE, vemr)])
                        self.dma("gpsimd", self.dbg["d_kb"].rearrange("c p t -> p c t"),
                                 k_fm.rearrange("p (c t) -> p c t", c=NP), self.sGP)
                        self.op("gpsimd", lambda e, v=self.sST.v: e.wait_ge(self.sST.h, v))
                        self.dma("gpsimd", self.dbg["d_vb"], self.vb, self.sGP)
                        self.dma("gpsimd", self.dbg["d_mb"], mrb, self.sGP,
                                 [(self.sVE, vemr)])
                    self.flush()
                eT = sba([128, KT * 512], BF16, "eT")
                vbf = sba([128, 2 * 512], BF16, "vbf")
                of = sba([128, 8 * 512], F32, "of")
                srow = of[0:1, 0:512]
                rc4 = sba([128, 4], F32, "rc4")
                xa = sba([128, D], F32, "xa")
                xn = self.scr
                lnt = self.sq
                spl = self.scr2.bitcast(BF16)[:, 0:NP * 128]
                trf = MK.TrPool(self, ps_tf, 512, self.ident_f, nslots=2)
                trb = MK.TrPool(self, ps_tb, 1024, self.ident_b, nslots=2)
                spl_dma = None
                xn_marks = {}
                pv_evac = None
                sum_evac = None
                for qc in range(NC):
                    ktmax = 4 * qc + 4
                    exp_prev = None
                    mask_marks = {}
                    ldxs = {}
                    exp_slot = {}
                    for kt in range(ktmax):
                        sl_ = (kt % 2) * 512
                        wts = []
                        if exp_slot.get(kt % 2) is not None:
                            wts.append((self.sAC, exp_slot[kt % 2]))
                        if kt == 0 and pv_evac is not None:
                            wts.append((self.sAC, pv_evac))
                        for pp in range(NP):
                            mmq = self.op("tensor", lambda e, pp=pp, kt=kt, qc=qc, sl_=sl_: e.matmul(
                                ps_s[:, sl_:sl_ + 512],
                                k_fm[:, pp * T + kt * 128: pp * T + kt * 128 + 128],
                                q_fm[:, pp * T + qc * 512: pp * T + (qc + 1) * 512],
                                start=(pp == 0), stop=False, skip_group_check=True),
                                wts if pp == 0 else (), self.sMM)
                        mmq = self.op("tensor", lambda e, qc=qc, sl_=sl_: e.matmul(
                            ps_s[:, sl_:sl_ + 512], self.ones_row,
                            mrb[0:1, qc * 512:(qc + 1) * 512],
                            start=False, stop=True, skip_group_check=True),
                            [(self.sVE, vemr)], self.sMM)
                        exp_prev = self.op("scalar", lambda e, kt=kt, sl_=sl_: e.activation(
                            eT[:, kt * 512:(kt + 1) * 512], ps_s[:, sl_:sl_ + 512], AF.Exp),
                            [(self.sMM, mmq)], self.sAC)
                        exp_slot[kt % 2] = exp_prev
                        i_d = kt - 4 * qc
                        if i_d >= 0:
                            for sb4 in range(4):
                                delta = sb4 - i_d
                                slc = eT[:, kt * 512 + sb4 * 128: kt * 512 + (sb4 + 1) * 128]
                                if delta < 0:
                                    self.op("vector", lambda e, slc=slc: e.tensor_scalar(
                                        slc, slc, self.cst[:, 10:11], None, ALU.mult),
                                        [(self.sAC, exp_prev)], self.sVE)
                                elif delta == 0:
                                    self.op("vector", lambda e, slc=slc: e.tensor_tensor(
                                        slc, slc, self.causal_b, ALU.mult),
                                        [(self.sAC, exp_prev)], self.sVE)
                            mask_marks[kt] = self.sVE.v
                    for kt in range(ktmax):
                        wts = [(self.sAC, exp_prev), (self.sVE, mask_marks.get(kt))]
                        if kt == 0:
                            if trf.ev_marks and trf.ev_marks[-1] is not None:
                                wts.append(trf.ev_marks[-1])
                        mms = self.op("tensor", lambda e, kt=kt: e.matmul(
                            ps_sm[0:1, 0:512], self.ones_col,
                            eT[:, kt * 512:(kt + 1) * 512],
                            start=(kt == 0), stop=(kt == ktmax - 1),
                            skip_group_check=True), wts, self.sMM)
                    sum_evac = self.op("vector", lambda e: e.tensor_copy(
                        srow, ps_sm[0:1, 0:512]), [(self.sMM, mms)], self.sVE)
                    for i in range(4):
                        dst_ps, mm, idx = trf.tr(srow[0:1, i * 128:(i + 1) * 128],
                                                 [(self.sVE, sum_evac)])
                        ver = self.op("vector", lambda e, d=dst_ps, i=i: e.reciprocal(
                            rc4[:, i:i + 1], d), [(self.sMM, mm)], self.sVE)
                        trf.done(idx, (self.sVE, ver))
                    for qtr in range(4):
                        vload = None
                        for kt in range(ktmax):
                            bv = kt % 2
                            wts = list(vb_snap) if (qc == 0 and qtr == 0 and kt == 0) else []
                            pm = mask_marks.get(("vbl", bv))
                            if pm is not None:
                                wts.append((self.sMM, pm))
                            vload = self.sload("vbf", kt, "sync",
                                               vbf[:, bv * 256:(bv + 1) * 256],
                                               self.vb[kt * 128:(kt + 1) * 128,
                                                       qtr * 256:(qtr + 1) * 256], wts)
                            for dcq in range(2):
                                wts2 = [vload, (self.sVE, mask_marks.get(kt) if isinstance(mask_marks.get(kt), int) else None)]
                                if kt == 0 and pv_evac is not None:
                                    wts2.append((self.sAC, pv_evac))
                                mmp = self.op("tensor", lambda e, kt=kt, dcq=dcq, bv=bv: e.matmul(
                                    ps_pv[:, dcq * 512:(dcq + 1) * 512],
                                    vbf[:, bv * 256 + dcq * 128: bv * 256 + (dcq + 1) * 128],
                                    eT[:, kt * 512:(kt + 1) * 512],
                                    start=(kt == 0), stop=(kt == ktmax - 1),
                                    skip_group_check=True),
                                    wts2 if dcq == 0 else (), self.sMM)
                            mask_marks[("vbl", bv)] = mmp
                        for dcq in range(2):
                            pv_evac = self.op("scalar", lambda e, qtr=qtr, dcq=dcq: e.activation(
                                of[:, (qtr * 2 + dcq) * 512:(qtr * 2 + dcq + 1) * 512],
                                ps_pv[:, dcq * 512:(dcq + 1) * 512], AF.Identity),
                                [(self.sMM, mmp)], self.sAC)
                    for ts_ in range(4):
                        tt = qc * 4 + ts_
                        wts = [(self.sVE, xn_marks.get(tt - 1))]
                        ldx = self.sload("xa1", tt, "sync", xa,
                                         self.xres[tt * 128:(tt + 1) * 128, :], wts)
                        for oc in range(NP):
                            dst_ps, mm, idx = trf.tr(
                                of[:, oc * 512 + ts_ * 128: oc * 512 + (ts_ + 1) * 128],
                                [(self.sAC, pv_evac)])
                            out_ap = xn.rearrange("p (di h) -> p h di", h=NH)[:, 2 * oc:2 * oc + 2, :]
                            in_ap = xa.rearrange("p (di h) -> p h di", h=NH)[:, 2 * oc:2 * oc + 2, :]
                            wxn = [(self.sMM, mm), ldx]
                            if oc == 0 and self._xst.get(tt - 1) is not None:
                                wxn.append(self._xst[tt - 1])
                            vex = self.op("vector", lambda e, d=dst_ps, o=out_ap, i_=in_ap,
                                          ts_=ts_: e.scalar_tensor_tensor(
                                o, d.rearrange("p (hh di) -> p hh di", hh=2),
                                rc4[:, ts_:ts_ + 1], i_, ALU.mult, ALU.add),
                                wxn, self.sVE)
                            trf.done(idx, (self.sVE, vex))
                        self._xst[tt] = self.sload("xst", tt, "sync",
                                                   self.xres[tt * 128:(tt + 1) * 128, :],
                                                   xn, [(self.sVE, vex)])
                        ve_ln = self.ln_tile(xn, lnt, extra_src_waits=[(self.sVE, vex)])
                        xn_marks[tt] = ve_ln
                        for dc in range(NP):
                            wts2 = [(self.sVE, ve_ln)]
                            if dc == 0 and spl_dma is not None:
                                wts2.append(spl_dma)
                            dst_ps, mm, idx = trb.tr(lnt[:, dc * 128:(dc + 1) * 128], wts2)
                            ev2 = self.op("vector", lambda e, d=dst_ps, dc=dc: e.tensor_copy(
                                spl[:, dc * 128:(dc + 1) * 128], d), [(self.sMM, mm)], self.sVE)
                            trb.done(idx, (self.sVE, ev2))
                        spl_dma = self.sload(
                            "spl", tt, "sync",
                            self.lnfm.rearrange("p d t -> d p t")[:, :, tt * 128:(tt + 1) * 128],
                            spl.rearrange("d (p t) -> d p t", p=NP), [(self.sVE, ev2)])
                self.flush()
                if self.debug:
                    self.op("gpsimd", lambda e, v=self.sST.v: e.wait_ge(self.sST.h, v))
                    self.dma("gpsimd", self.dbg["d_x3"], self.xres, self.sGP)
                    self.dma("gpsimd", self.dbg["d_lnfm"], self.lnfm, self.sGP)
                    self.flush()
            self.serial = _sv

    def finalize(self):
        self.dma_drain(engines=("sync",))
        self.flush()


def _rope_tables(T):
    half = D // 2
    inv = 1.0 / (ROPE_BASE ** (np.arange(half, dtype=np.float64) / half))
    ang = np.arange(T, dtype=np.float64)[:, None] * inv[None, :]
    cos, sin = np.cos(ang), np.sin(ang)
    cosP = np.zeros((NP, HD, T), np.float32)
    sinP = np.zeros((NP, HD, T), np.float32)
    for p in range(NP):
        for r in range(HD):
            i = 8 * r + p
            cosP[p, r, :] = cos[:, i]
            sinP[p, r, :] = sin[:, i]
    return cosP, sinP


def make_inputs(full, T):
    causal = np.triu(np.ones((128, 128), np.float32))
    cosP, sinP = _rope_tables(T)
    m = dict(
        Wq=full["L0_attn_a_Wq"], Wk=full["L0_attn_a_Wk"], Wv=full["L0_attn_a_Wv"],
        Wo=full["L0_attn_a_Wo"], qg=full["L0_attn_a_q_gamma"], qb=full["L0_attn_a_q_beta"],
        kg=full["L0_attn_a_k_gamma"], kb=full["L0_attn_a_k_beta"],
        wq3=np.ascontiguousarray(np.transpose(full["L0_attn_b_w_q"], (2, 0, 1))),
        wk3=np.ascontiguousarray(np.transpose(full["L0_attn_b_w_k"], (2, 0, 1))),
        wv3=np.ascontiguousarray(np.transpose(full["L0_attn_b_w_v"], (2, 0, 1))),
        rw=full["L0_ffn_router_w"], sg=full["L0_ffn_shared_gate_w"],
        su=full["L0_ffn_shared_up_w"], sd=full["L0_ffn_shared_down_w"],
        eg=full["L0_ffn_expert_gate_k"], eu=full["L0_ffn_expert_up_k"],
        ed=full["L0_ffn_expert_down_k"],
        identf=np.eye(128, dtype=np.float32), causalf=causal,
        cosP=cosP, sinP=sinP,
    )
    return {k: np.ascontiguousarray(np.asarray(v, dtype=np.float32)) for k, v in m.items()}


def kernel(**inputs):
    T = 4096
    mk = MK(T=T)
    nc = mk.build()
    base = make_inputs(inputs, T)
    x = np.ascontiguousarray(np.asarray(inputs["L0_attn_a_x"], dtype=np.float32))
    in_maps = [dict(base, x=x[c]) for c in range(8)]
    res = run_bass_kernel_spmd(nc, in_maps, core_ids=list(range(8)))
    return np.stack([r["y"] for r in res.results], axis=0)
